# version 08 (final)
# baseline (speedup 1.0000x reference)
import math

import numpy as np

import concourse.bass as bass
import concourse.mybir as mybir
from concourse.bass_utils import run_bass_kernel_spmd

F32 = mybir.dt.float32
BF16 = mybir.dt.bfloat16
AX = mybir.AxisListType
ACT = mybir.ActivationFunctionType
OP = mybir.AluOpType

ENGS = ("sync", "tensor", "vector", "scalar", "gpsimd")

D = 1024
NH = 16
DH = 64
M2 = 64
DFF = 4096
NE = 8
T = 512
C = 128
LN_EPS = 1e-5
ATTN_EPS = 1e-6


class Res:
    __slots__ = ("ws", "rs")

    def __init__(self):
        self.ws = {}
        self.rs = {}

    def rdeps(self):
        return list(self.ws.items())

    def wdeps(self):
        return list(self.ws.items()) + list(self.rs.items())

    def did_read(self, ev):
        e, v = ev
        if self.rs.get(e, 0) < v:
            self.rs[e] = v

    def did_write(self, ev):
        e, v = ev
        if self.ws.get(e, 0) < v:
            self.ws[e] = v


class Prog:
    def __init__(self, nc):
        self.nc = nc
        self.q = {e: [] for e in ENGS}
        self.sem = {e: nc.alloc_semaphore("prog_" + e) for e in ENGS}
        self.cnt = {e: 0 for e in ENGS}
        self.lastwait = {e: {} for e in ENGS}
        self.filler = []
        self.fill_credit = 0.0

    def fill(self, credit):
        if not self.filler:
            return
        self.fill_credit += credit
        while self.fill_credit >= 1.0 and self.filler:
            self.fill_credit -= 1.0
            self.filler.pop(0)()

    def flush_filler(self):
        while self.filler:
            self.filler.pop(0)()

    def stream(self, name):
        self.sem[name] = self.nc.alloc_semaphore("dma_" + name)
        self.cnt[name] = 0

    def _deps(self, eng, reads, writes):
        need = {}
        for r in reads:
            for e2, v in r.rdeps():
                if need.get(e2, 0) < v:
                    need[e2] = v
        for w in writes:
            for e2, v in w.wdeps():
                if need.get(e2, 0) < v:
                    need[e2] = v
        for e2, v in need.items():
            if eng == "tensor" and e2 == "tensor":
                continue
            if self.lastwait[eng].get(e2, 0) < v:
                self.lastwait[eng][e2] = v
                s = self.sem[e2]
                self.q[eng].append(
                    lambda E, s=s, v=v: E.wait_ge(s, v))

    def op(self, eng, fn, reads=(), writes=(), dma=None):
        self._deps(eng, reads, writes)
        key = eng if dma is None else dma
        if dma is not None and self.cnt.get(key, 0) > 0:
            if self.lastwait[eng].get(key, 0) < self.cnt[key]:
                self.lastwait[eng][key] = self.cnt[key]
                s0, v0 = self.sem[key], self.cnt[key]
                self.q[eng].append(
                    lambda E, s0=s0, v0=v0: E.wait_ge(s0, v0))
        k = 1 if dma is None else 16
        self.cnt[key] += k
        s = self.sem[key]
        self.q[eng].append(lambda E, fn=fn, s=s, k=k: fn(E).then_inc(s, k))
        ev = (key, self.cnt[key])
        for r in reads:
            r.did_read(ev)
        for w in writes:
            w.did_write(ev)
        return ev

    def group(self, eng, fns, reads=(), writes=()):
        self._deps(eng, reads, writes)
        self.cnt[eng] += 1
        s = self.sem[eng]
        for fn in fns[:-1]:
            self.q[eng].append(lambda E, fn=fn: fn(E))
        fn = fns[-1]
        self.q[eng].append(lambda E, fn=fn, s=s: fn(E).then_inc(s, 1))
        ev = (eng, self.cnt[eng])
        for r in reads:
            r.did_read(ev)
        for w in writes:
            w.did_write(ev)
        return ev

    def replay(self):
        nc = self.nc
        with nc.Block() as blk:
            @blk.sync
            def _(E):
                for fn in self.q["sync"]:
                    fn(E)

            @blk.tensor
            def _(E):
                for fn in self.q["tensor"]:
                    fn(E)

            @blk.vector
            def _(E):
                for fn in self.q["vector"]:
                    fn(E)

            @blk.scalar
            def _(E):
                for fn in self.q["scalar"]:
                    fn(E)

            @blk.gpsimd
            def _(E):
                for fn in self.q["gpsimd"]:
                    fn(E)


def build_program(S_len=4096, L=3, with_moe=True, detect_races=False,
                  with_attn=True, attn_stop=None, skip=()):
    assert S_len % T == 0
    NT = S_len // T
    NSUB = T // 128
    NCH = 8
    NSLOT = 4

    nc = bass.Bass("TRN2", target_bir_lowering=False,
                   detect_race_conditions=detect_races)

    d_in = {}

    def din(name, shape, dt=F32):
        d_in[name] = nc.dram_tensor(name, list(shape), dt,
                                    kind="ExternalInput").ap()
        return d_in[name]

    x_d = din("x", [S_len, D])
    wq_d = din("Wq", [L, D, D])
    wk_d = din("Wk", [L, D, D])
    wv_d = din("Wv", [L, D, D])
    wo_d = din("Wo", [L, D, D])
    bq_d = din("bq", [L, D])
    bk_d = din("bk", [L, D])
    bv_d = din("bv", [L, D])
    bo_d = din("bo", [L, D])
    om_d = din("omega", [L, DH, 32])
    l1g_d = din("ln1_g", [L, D])
    l1b_d = din("ln1_b", [L, D])
    l2g_d = din("ln2_g", [L, D])
    l2b_d = din("ln2_b", [L, D])
    w1_d = din("W1", [L, D, DFF])
    bf1_d = din("bf1", [L, DFF])
    w2_d = din("W2", [L, DFF, D])
    bf2_d = din("bf2", [L, D])
    gw_d = din("gate_w", [NE, D])
    m1_d = din("mw1", [DFF, NE, D])
    mb1_d = din("mb1", [NE, DFF])
    m2_d = din("mw2", [D, NE, DFF])
    mb2_d = din("mb2", [NE, D])
    idb_d = din("ident_bf", [128, 128], BF16)
    idf_d = din("ident_f32", [128, 128])
    um_d = din("umask", [128, 128])

    out_d = nc.dram_tensor("out", [S_len, D], F32, kind="ExternalOutput").ap()

    wq_s = nc.dram_tensor("wq_s", [L, NCH, 128, D], BF16, kind="Internal").ap()
    wk_s = nc.dram_tensor("wk_s", [L, NCH, 128, D], BF16, kind="Internal").ap()
    wv_s = nc.dram_tensor("wv_s", [L, NCH, 128, D], BF16, kind="Internal").ap()
    wo_s = nc.dram_tensor("wo_s", [L, NCH, 128, D], BF16, kind="Internal").ap()
    w1_s = nc.dram_tensor("w1_s", [L, 4, NCH, 128, D], BF16, kind="Internal").ap()
    w2_s = nc.dram_tensor("w2_s", [L, 4, NCH, 128, D], BF16, kind="Internal").ap()
    m1_s = nc.dram_tensor("m1_s", [NE, 4, NCH, 128, D], BF16, kind="Internal").ap()
    m2_s = nc.dram_tensor("m2_s", [NE, 4, NCH, 128, D], BF16, kind="Internal").ap()
    ln_s = nc.dram_tensor("ln_s", [L, 4, 128, D], F32, kind="Internal").ap()

    def sb(name, shape, dt):
        return nc.alloc_sbuf_tensor(name, list(shape), dt).ap()

    stream = sb("stream", [128, NSUB * D], F32)
    y2acc = sb("y2acc", [128, NSUB * D], F32)
    xn_fm = sb("xn_fm", [128, NCH * T], BF16)
    sfm = sb("sfm", [128, NCH * T], BF16)
    a_fm = sb("a_fm", [128, NCH * T], BF16)
    qf = sb("qf", [128, NCH * T], BF16)
    kf = sb("kf", [128, NCH * T], BF16)
    kft = sb("kft", [128, NSUB * D], BF16)
    a_tm = sb("a_tm", [128, NSUB * D], BF16)
    v_tm = sb("v_tm", [128, NSUB * NH * 65], BF16)
    y1 = sb("y1", [128, NCH * T], BF16)
    q2 = sb("q2", [128, T], BF16)
    s_sc = sb("s_sc", [128, T], F32)
    wslot = sb("wslot", [128, NSLOT * NCH * D], BF16)
    lnrep = sb("lnrep", [128, 2 * D], F32)
    xr = sb("xr", [128, 512], F32)
    sst = sb("sst", [128, L * NH // 2 * 65], F32)
    saug = sb("saug", [128, NH // 2 * 66], BF16)
    gmb = sb("gmb", [128, 2 * 128], BF16)
    ndb = sb("ndb", [128, 2 * 66], F32)
    rcb = sb("rcb", [128, 8], F32)
    stats = sb("stats", [128, NSUB * 24], F32)
    moeps = sb("moeps", [128, 2 * 512], F32)
    wtok = sb("wtok", [128, NSUB * 8], F32)
    prb = sb("prb", [128, 2 * 8], F32)
    wT = sb("wT", [8, 128], F32)
    b2r = sb("b2r", [8, D], F32)
    bqk = sb("bqk", [128, L * 16], F32)
    bf1r = sb("bf1r", [128, L * 32], F32)
    b1r = sb("b1r", [128, NE * 32], F32)
    rowb = [sb("rowb0", [1, D], F32)]
    ones1 = sb("ones1", [1, 128], F32)
    gwT = sb("gwT", [128, NCH * 8], BF16)
    gwT32 = sb("gwT32", [128, NCH * 8], F32)
    omblk = sb("omblk", [128, L * 128], BF16)
    onesblk = sb("onesblk", [128, 128], BF16)
    identb = sb("identb", [128, 128], BF16)
    identf = sb("identf", [128, 128], F32)
    umask = sb("umask_sb", [128, 128], F32)
    omsb = sb("omsb", [128, 32], F32)
    cns = sb("cns", [128, 8], F32)

    psA = nc.alloc_psum_tensor("psA", [128, 512], F32).ap()
    psB = nc.alloc_psum_tensor("psB", [128, 512], F32).ap()
    psT0 = nc.alloc_psum_tensor("psT0", [128, 128], BF16).ap()
    psT1 = nc.alloc_psum_tensor("psT1", [128, 128], BF16).ap()
    psG0 = nc.alloc_psum_tensor("psG0", [128, 128], F32).ap()
    psG1 = nc.alloc_psum_tensor("psG1", [128, 128], F32).ap()
    psN0 = nc.alloc_psum_tensor("psN0", [128, 160], F32).ap()
    psN1b = nc.alloc_psum_tensor("psN1b", [128, 160], F32).ap()
    psT_banks = [psT0, psT1]
    psG_banks = [psG0, psG1]
    psN_banks = [psN0, psN1b]

    P = Prog(nc)
    for _s in ("ws0", "ws1", "ws2", "ws3", "p0a", "p0b", "p0c", "p0o",
               "p0lno", "cload", "lnld", "rowld", "xld", "xrld", "outst"):
        P.stream(_s)

    r_x = Res()
    r_stream = [Res() for _ in range(NSUB)]
    r_y2 = [[Res() for _ in range(2)] for _ in range(NSUB)]
    r_xnfm = [Res() for _ in range(NCH)]
    r_sfm = [Res() for _ in range(NCH)]
    r_afm = [Res() for _ in range(NCH)]
    r_qf = [Res() for _ in range(NCH)]
    r_kf = [Res() for _ in range(NCH)]
    r_kft = [Res() for _ in range(NSUB)]
    r_atm = [Res() for _ in range(NSUB)]
    r_vtm = [[Res() for _ in range(2)] for _ in range(NSUB)]
    r_y1 = [Res() for _ in range(NCH)]
    r_q2 = [Res() for _ in range(2)]
    r_ssc = [Res() for _ in range(2)]
    r_slot = [Res() for _ in range(NSLOT)]
    r_lnrep = Res()
    r_xr = Res()
    r_sst = [Res() for _ in range(NH)]
    r_saug = [Res() for _ in range(NH // 2)]
    r_gmb = [Res() for _ in range(2)]
    r_ndb = [Res() for _ in range(2)]
    r_rcb = [Res() for _ in range(2)]
    r_stats = [Res() for _ in range(NSUB)]
    r_moeps = [Res(), Res()]
    r_wtok = [Res() for _ in range(NSUB)]
    r_prb = [Res() for _ in range(2)]
    _rwT = Res()
    r_wT = [_rwT for _ in range(NSUB)]
    r_psA = Res()
    r_psB = Res()
    r_psG = [Res() for _ in range(2)]
    r_psN1 = [Res() for _ in range(2)]
    r_psN2 = [Res() for _ in range(2)]
    r_psT = [Res() for _ in range(2)]
    r_const = Res()
    r_rowb = [Res()]
    rowb_i = [0]

    def load_row(src_row_ap):
        i = 0
        P.op("gpsimd", lambda E, i=i, a=src_row_ap: E.dma_start(
            out=rowb[i], in_=a), writes=[r_rowb[i]], dma="rowld")
        return rowb[i], r_rowb[i]
    r_out = Res()
    r_wq_s = [Res() for _ in range(L)]
    r_wk_s = [Res() for _ in range(L)]
    r_wv_s = [Res() for _ in range(L)]
    r_wo_s = [Res() for _ in range(L)]
    r_w1_s = [[Res() for _ in range(4)] for _ in range(L)]
    r_w2_s = [[Res() for _ in range(4)] for _ in range(L)]
    r_m1_s = [Res() for _ in range(NE)]
    r_m2_s = [Res() for _ in range(NE)]
    r_ln_s = [[Res() for _ in range(4)] for _ in range(L)]
    r_ph_f32 = [Res() for _ in range(3)]
    r_ph_bf = [Res() for _ in range(4)]

    big_ps = [(psA, r_psA), (psB, r_psB)]
    ps_i = [0]

    def next_ps(n=1):
        out = []
        for _ in range(n):
            out.append(big_ps[ps_i[0] % 2])
            ps_i[0] += 1
        return out if n > 1 else out[0]

    rot = {"G": [0], "N1": [0], "N2": [0], "T": [0], "q2": [0],
           "ssc": [0], "gmb": [0], "ndb": [0], "rcb": [0], "prb": [0],
           "moeps": [0]}

    def nxt(key, n):
        i = rot[key][0] % n
        rot[key][0] += 1
        return i

    ev0 = P.op("vector", lambda E: E.memset(ones1, 1.0), writes=[r_const])
    P.op("vector", lambda E: E.memset(cns[:, 0:1], 64.0), writes=[r_const])
    P.op("scalar", lambda E: E.activation(out=cns[:, 1:2], in_=cns[:, 0:1],
                                          func=ACT.Ln),
         reads=[r_const], writes=[r_const])
    P.op("vector", lambda E: E.tensor_scalar_mul(out=cns[:, 2:3],
                                                 in0=cns[:, 1:2], scalar1=-0.5),
         reads=[r_const], writes=[r_const])
    P.op("vector", lambda E: E.tensor_scalar_mul(out=cns[:, 3:4],
                                                 in0=cns[:, 1:2],
                                                 scalar1=-0.25),
         reads=[r_const], writes=[r_const])
    P.op("scalar", lambda E: E.activation(out=cns[:, 4:5], in_=cns[:, 3:4],
                                          func=ACT.Exp),
         reads=[r_const], writes=[r_const])
    P.op("gpsimd", lambda E: E.dma_start(out=identb, in_=idb_d),
         writes=[r_const], dma="cload")
    P.op("gpsimd", lambda E: E.dma_start(out=identf, in_=idf_d),
         writes=[r_const], dma="cload")
    P.op("gpsimd", lambda E: E.dma_start(out=umask, in_=um_d),
         writes=[r_const], dma="cload")
    for l in range(L):
        for (dd, base) in ((bq_d, 0), (bk_d, 8)):
            P.op("gpsimd", lambda E, dd=dd, l=l, base=base: E.dma_start(
                out=bqk[:, l * 16 + base: l * 16 + base + 8],
                in_=dd[l].rearrange("(c p) -> p c", p=128),
                allow_slow_non_contiguous=True),
                writes=[r_const], dma="cload")
        P.op("gpsimd", lambda E, l=l: E.dma_start(
            out=bf1r[:, l * 32:(l + 1) * 32],
            in_=bf1_d[l].rearrange("(c p) -> p c", p=128),
            allow_slow_non_contiguous=True),
            writes=[r_const], dma="cload")
    if with_moe:
        for e in range(NE):
            P.op("gpsimd", lambda E, e=e: E.dma_start(
                out=b1r[:, e * 32:(e + 1) * 32],
                in_=mb1_d[e].rearrange("(c p) -> p c", p=128),
                allow_slow_non_contiguous=True),
                writes=[r_const], dma="cload")
        P.op("gpsimd", lambda E: E.dma_start(out=b2r, in_=mb2_d),
             writes=[r_const], dma="cload")
        for c in range(NCH):
            P.op("gpsimd", lambda E, c=c: E.dma_start(
                out=gwT32[:, c * 8:(c + 1) * 8],
                in_=gw_d[:, c * 128:(c + 1) * 128].transpose([1, 0]),
                allow_slow_non_contiguous=True),
                writes=[r_const], dma="cload")
        P.op("vector", lambda E: E.tensor_copy(out=gwT, in_=gwT32),
             reads=[r_const], writes=[r_const])
    for l in range(L):
        P.op("gpsimd", lambda E, l=l: E.dma_start(out=omsb[0:64, :],
                                                  in_=om_d[l]),
             writes=[r_const], dma="cload")
        P.op("vector", lambda E: E.tensor_copy(out=omsb[64:128, :],
                                               in_=omsb[0:64, :]),
             reads=[r_const], writes=[r_const])
        stg = stream[:, 0:128]
        P.op("vector", lambda E, stg=stg: E.memset(stg, 0.0),
             writes=[r_ph_f32[1]])
        for (p0, p1, c0, neg) in ((0, 64, 0, False), (0, 64, 32, True),
                                  (64, 128, 64, False), (64, 128, 96, True)):
            sc = cns[:, 5:6] if neg else cns[:, 4:5]
            if neg:
                P.op("vector", lambda E: E.tensor_scalar_mul(
                    out=cns[:, 5:6], in0=cns[:, 4:5], scalar1=-1.0),
                    reads=[r_const], writes=[r_const])
            P.op("vector", lambda E, p0=p0, p1=p1, c0=c0, sc=sc, stg=stg:
                 E.tensor_scalar_mul(out=stg[p0:p1, c0:c0 + 32],
                                     in0=omsb[p0:p1, :], scalar1=sc[p0:p1, :]),
                 reads=[r_const], writes=[r_ph_f32[1]])
        P.op("vector", lambda E, l=l, stg=stg: E.tensor_copy(
            out=omblk[:, l * 128:(l + 1) * 128], in_=stg),
            reads=[r_ph_f32[1]], writes=[r_const])
    P.op("vector", lambda E: E.memset(onesblk, 0.0), writes=[r_const])
    P.op("vector", lambda E: E.memset(onesblk[0:64, 0:64], -0.0625),
         writes=[r_const])
    P.op("vector", lambda E: E.memset(onesblk[64:128, 64:128], -0.0625),
         writes=[r_const])
    for sub in range(NSUB):
        base = sub * NH * 65
        ap = v_tm[:, base:base + NH * 65].rearrange("p (h c) -> p h c", h=NH)
        P.op("vector", lambda E, ap=ap: E.memset(ap[:, :, 64:65], 1.0),
             writes=[r_vtm[sub][0], r_vtm[sub][1]])
    P.op("vector", lambda E: E.memset(sst, 0.0), writes=r_sst)

    do_cast = "cast" not in skip
    ph_f32 = [(stream, r_ph_f32[1], "p0b", "gpsimd"),
              (y2acc, r_ph_f32[2], "p0c", "sync")]
    ph_bf = [(sfm, r_ph_bf[0]), (a_fm, r_ph_bf[1]), (qf, r_ph_bf[2]),
             (kf, r_ph_bf[3])]
    ph_mc = [(y1, r_y1[0]), (kft, r_kft[0]), (a_tm, r_atm[0])]
    ph_i = [0]
    bf_i = [0]
    mc_i = [0]

    def f32buf():
        i = ph_i[0] % 2
        ph_i[0] += 1
        return ph_f32[i]

    def bfbuf():
        i = bf_i[0] % 4
        bf_i[0] += 1
        return ph_bf[i]

    def mcbuf():
        i = mc_i[0] % 3
        mc_i[0] += 1
        return ph_mc[i]

    def cast_block(in_view, out_view, ncols, scr_res):
        fb, fres, fstr, fq = f32buf()
        bb, bres = bfbuf()
        n = ncols * 1024
        fv = fb[:, 0:n].rearrange("p (c f) -> p c f", c=ncols)
        bv = bb[:, 0:n].rearrange("p (c f) -> p c f", c=ncols)
        P.op(fq, lambda E, fv=fv, in_view=in_view: E.dma_start(
            out=fv, in_=in_view), writes=[fres], dma=fstr)
        P.op("vector", lambda E, bb=bb, fb=fb, n=n: E.tensor_copy(
            out=bb[:, 0:n], in_=fb[:, 0:n]), reads=[fres], writes=[bres])
        P.op("scalar", lambda E, out_view=out_view, bv=bv: E.dma_start(
            out=out_view, in_=bv), reads=[bres], writes=[scr_res],
            dma="p0o")

    def conv_attn_layer(l):
        for (src_, scr, rres) in ((wq_d, wq_s, r_wq_s), (wk_d, wk_s, r_wk_s),
                                  (wv_d, wv_s, r_wv_s), (wo_d, wo_s, r_wo_s)):
            for dc in range(0, NCH, 4):
                cast_block(
                    src_[l, dc * 128:(dc + 4) * 128, :]
                    .rearrange("(c p) f -> p c f", c=4),
                    scr[l, dc:dc + 4].transpose([1, 0, 2]),
                    4, rres[l])
        for dc in range(NCH):
            cast_block(
                w1_d[l, dc * 128:(dc + 1) * 128, :]
                .rearrange("p (c f) -> p c f", c=4),
                w1_s[l, :, dc].transpose([1, 0, 2]),
                4, r_w1_s[l][0])
        for j in range(NCH):
            cast_block(
                w2_d[l, j * 512:(j + 1) * 512, :]
                .rearrange("(c p) f -> p c f", c=4),
                w2_s[l].rearrange("a b p f -> (a b) p f")[j * 4:(j + 1) * 4]
                .transpose([1, 0, 2]),
                4, r_w2_s[l][0])

    tp_i = [0]

    def tp_slot():
        i = tp_i[0] % 2
        tp_i[0] += 1
        return psT_banks[i], r_psT[i]

    def conv_moe_expert(e):
        for fc in range(DFF // 128):
            fb_, fres, fstr, fq = f32buf()
            bb_, bres = bfbuf()
            mb_, mres = mcbuf()
            P.op(fq, lambda E, fc=fc, e=e, fb_=fb_: E.dma_start(
                out=fb_[:, 0:D], in_=m1_d[fc * 128:(fc + 1) * 128, e, :]),
                writes=[fres], dma=fstr)
            P.op("vector", lambda E, fb_=fb_, mb_=mb_: E.tensor_copy(
                out=mb_[:, 0:D], in_=fb_[:, 0:D]),
                reads=[fres], writes=[mres])
            for dc in range(NCH):
                ps, pres = tp_slot()
                P.op("tensor", lambda E, ps=ps, dc=dc, mb_=mb_:
                     E.transpose(out=ps,
                                 in_=mb_[:, dc * 128:(dc + 1) * 128],
                                 identity=identb),
                     reads=[mres, r_const], writes=[pres])
                P.op("vector", lambda E, bb_=bb_, dc=dc, ps=ps:
                     E.tensor_copy(out=bb_[:, dc * 128:(dc + 1) * 128],
                                   in_=ps),
                     reads=[pres], writes=[bres])
            ov = m1_s[e, fc // 8, :, :, (fc % 8) * 128:(fc % 8 + 1) * 128]\
                .transpose([1, 0, 2])
            iv = bb_[:, 0:1024].rearrange("p (c f) -> p c f", c=8)
            P.op("scalar", lambda E, ov=ov, iv=iv: E.dma_start(
                out=ov, in_=iv), reads=[bres], writes=[r_m1_s[e]],
                dma="p0o")
        for dc in range(NCH):
            fb_, fres, fstr, fq = f32buf()
            bb_, bres = bfbuf()
            mb_, mres = mcbuf()
            P.op(fq, lambda E, dc=dc, e=e, fb_=fb_: E.dma_start(
                out=fb_[:, 0:DFF],
                in_=m2_d[dc * 128:(dc + 1) * 128, e, :]),
                writes=[fres], dma=fstr)
            P.op("vector", lambda E, fb_=fb_, mb_=mb_: E.tensor_copy(
                out=mb_[:, 0:DFF], in_=fb_[:, 0:DFF]),
                reads=[fres], writes=[mres])
            for fc in range(DFF // 128):
                ps, pres = tp_slot()
                P.op("tensor", lambda E, ps=ps, fc=fc, mb_=mb_:
                     E.transpose(out=ps,
                                 in_=mb_[:, fc * 128:(fc + 1) * 128],
                                 identity=identb),
                     reads=[mres, r_const], writes=[pres])
                P.op("vector", lambda E, bb_=bb_, fc=fc, ps=ps:
                     E.tensor_copy(out=bb_[:, fc * 128:(fc + 1) * 128],
                                   in_=ps),
                     reads=[pres], writes=[bres])
            ov = m2_s[e, :, :, :, dc * 128:(dc + 1) * 128]\
                .transpose([2, 0, 1, 3])
            iv = bb_[:, 0:DFF].rearrange("p (a b f) -> p a b f", a=4, b=8)
            P.op("scalar", lambda E, ov=ov, iv=iv: E.dma_start(
                out=ov, in_=iv), reads=[bres], writes=[r_m2_s[e]],
                dma="p0o")

    lnrow = rowb[0]
    r_lnrow = r_rowb[0]
    for l in range(L if "lnrep" not in skip else 0):
        for (j, dd) in ((0, l1g_d), (1, l1b_d), (2, l2g_d), (3, l2b_d)):
            P.op("gpsimd", lambda E, dd=dd, l=l: E.dma_start(
                out=lnrow, in_=dd[l:l + 1, :]),
                writes=[r_lnrow], dma="rowld")
            fb_, fres, fstr, fq = f32buf()
            for h in range(2):
                P.op("tensor", lambda E, h=h: E.matmul(
                    out=(psA if h == 0 else psB)[:, 0:512], lhsT=ones1,
                    rhs=lnrow[:, h * 512:(h + 1) * 512],
                    start=True, stop=True),
                    reads=[r_lnrow, r_const],
                    writes=[r_psA if h == 0 else r_psB])
                P.op("vector", lambda E, h=h, fb_=fb_: E.tensor_copy(
                    out=fb_[:, h * 512:(h + 1) * 512],
                    in_=(psA if h == 0 else psB)[:, 0:512]),
                    reads=[r_psA if h == 0 else r_psB], writes=[fres])
            P.op("gpsimd", lambda E, l=l, j=j, fb_=fb_: E.dma_start(
                out=ln_s[l, j], in_=fb_[:, 0:D]),
                reads=[fres], writes=[r_ln_s[l][j]], dma="p0lno")

    conv_order = []
    for l in range(L if do_cast else 0):
        conv_order.append(("attn", l))
        if with_moe:
            if l == 0:
                conv_order += [("moe", 0), ("moe", 1)]
            elif l == 1:
                conv_order += [("moe", 2), ("moe", 3)]
    if with_moe:
        done = {e for k, e in conv_order if k == "moe"}
        conv_order += [("moe", e) for e in range(NE) if e not in done]
    for kind, idx in conv_order:
        if kind == "attn":
            conv_attn_layer(idx)
        else:
            conv_moe_expert(idx)
    for l in range(L):
        for fb in range(4):
            r_w1_s[l][fb] = r_w1_s[l][0]
            r_w2_s[l][fb] = r_w2_s[l][0]

    def _merge(phase, mains):
        for m in mains:
            for k, v in phase.ws.items():
                if m.rs.get(k, 0) < v:
                    m.rs[k] = v
                if m.ws.get(k, 0) < v:
                    m.ws[k] = v
            for k, v in phase.rs.items():
                if m.rs.get(k, 0) < v:
                    m.rs[k] = v
    _merge(r_ph_f32[1], r_stream)
    _merge(r_ph_f32[2], [r_y2[s][h] for s in range(NSUB) for h in range(2)])
    _merge(r_ph_bf[0], r_sfm)
    _merge(r_ph_bf[1], r_afm)
    _merge(r_ph_bf[2], r_qf)
    _merge(r_ph_bf[3], r_kf)
    _merge(r_y1[0], r_y1[1:])
    _merge(r_kft[0], r_kft[1:])
    _merge(r_atm[0], r_atm[1:])

    slot_i = [0]

    def wload(scr_view, scr_res):
        i = slot_i[0] % NSLOT
        slot_i[0] += 1
        sl = wslot[:, i * NCH * D:(i + 1) * NCH * D]
        res = r_slot[i]
        ov = sl.rearrange("p (c f) -> p c f", c=NCH)
        iv = scr_view.transpose([1, 0, 2])
        P.op("sync", lambda E, ov=ov, iv=iv: E.dma_start(
            out=ov[:, 0:NCH // 2, :], in_=iv[:, 0:NCH // 2, :]),
            reads=[scr_res], writes=[res], dma="ws%d" % i)
        P.op("sync", lambda E, ov=ov, iv=iv: E.dma_start(
            out=ov[:, NCH // 2:, :], in_=iv[:, NCH // 2:, :]),
            reads=[scr_res], writes=[res], dma="ws%d" % i)
        return sl, res

    def emit_ln(ap_in, ap_out, sub, res_in, res_out, gamma=None, beta=None,
               res_gb=None):
        st = stats[:, sub * 24:(sub + 1) * 24]
        sres = r_stats[sub]
        P.op("vector", lambda E: E.bn_stats(out=st[:, 0:6], in_=ap_in[:, 0:512]),
             reads=[res_in], writes=[sres])
        P.op("vector", lambda E: E.bn_stats(out=st[:, 6:12],
                                            in_=ap_in[:, 512:1024]),
             reads=[res_in], writes=[sres])
        P.op("vector", lambda E: E.bn_aggr(out=st[:, 12:14], in_=st[:, 0:12]),
             reads=[sres], writes=[sres])
        P.op("vector", lambda E: E.tensor_scalar_add(out=st[:, 14:15],
                                                     in0=st[:, 13:14],
                                                     scalar1=LN_EPS),
             reads=[sres], writes=[sres])
        P.op("vector", lambda E: E.reciprocal(out=st[:, 15:16],
                                              in_=st[:, 14:15]),
             reads=[sres], writes=[sres])
        P.op("scalar", lambda E: E.activation(out=st[:, 16:17],
                                              in_=st[:, 15:16], func=ACT.Sqrt),
             reads=[sres], writes=[sres])
        P.op("vector", lambda E: E.tensor_scalar(
            out=ap_out, in0=ap_in, scalar1=st[:, 12:13],
            scalar2=st[:, 16:17], op0=OP.subtract, op1=OP.mult),
            reads=[res_in, sres], writes=[res_out])
        if gamma is not None:
            P.op("gpsimd", lambda E: E.tensor_mul(out=ap_out, in0=ap_out,
                                                  in1=gamma),
                 reads=[res_gb, res_out], writes=[res_out])
            P.op("gpsimd", lambda E: E.tensor_add(out=ap_out, in0=ap_out,
                                                  in1=beta),
                 reads=[res_gb, res_out], writes=[res_out])
        P.fill(0.16)

    def cast_transpose(src_ap, src_res_list, dst_ap_chunks, dst_res,
                       scratch_ap, scratch_res):
        for sub in range(NSUB):
            cast_transpose_sub(src_ap, src_res_list, dst_ap_chunks, dst_res,
                               scratch_ap, scratch_res, sub)

    def cast_transpose_sub(src_ap, src_res_list, dst_ap_chunks, dst_res,
                           scratch_ap, scratch_res, sub):
        if True:
            P.op("vector", lambda E, sub=sub: E.tensor_copy(
                out=scratch_ap[:, sub * D:(sub + 1) * D],
                in_=src_ap[:, sub * D:(sub + 1) * D]),
                reads=[src_res_list[sub]], writes=[scratch_res[sub]])
            if "ctp_cast" in skip:
                return
            for p in range(NCH):
                i = nxt("T", 2)
                ps = psT_banks[i]
                pres = r_psT[i]
                P.op("tensor", lambda E, ps=ps, sub=sub, p=p: E.transpose(
                    out=ps,
                    in_=scratch_ap[:, sub * D + p * 128: sub * D + (p + 1) * 128],
                    identity=identb),
                    reads=[scratch_res[sub], r_const], writes=[pres])
                if "ctp_evac" in skip:
                    continue
                P.op("vector", lambda E, ps=ps, sub=sub, p=p: E.tensor_copy(
                    out=dst_ap_chunks[p][:, sub * 128:(sub + 1) * 128],
                    in_=ps),
                    reads=[pres], writes=[dst_res[p]])
                P.fill(0.033)

    def chunks_of(buf, n, w):
        return [buf[:, i * w:(i + 1) * w] for i in range(n)]

    xnfm_ch = chunks_of(xn_fm, NCH, T)
    sfm_ch = chunks_of(sfm, NCH, T)
    afm_ch = chunks_of(a_fm, NCH, T)
    qf_ch = chunks_of(qf, NCH, T)
    kf_ch = chunks_of(kf, NCH, T)
    y1_ch = chunks_of(y1, NCH, T)

    def proj_fm(dst_ch, dst_res, slot, slot_res, src_ch, src_res, bias_col):
        for pout in range(NCH):
            ps, pres = next_ps()
            fns = []
            for dc in range(NCH):
                fns.append(lambda E, ps=ps, slot=slot, dc=dc, pout=pout:
                           E.matmul(out=ps, lhsT=slot[:, dc * D + pout * 128:
                                                      dc * D + pout * 128 + 128],
                                    rhs=src_ch[dc],
                                    start=(dc == 0), stop=(dc == NCH - 1)))
            P.group("tensor", fns,
                    reads=[slot_res, r_const] + src_res, writes=[pres])
            P.op("scalar", lambda E, ps=ps, pout=pout: E.activation(
                out=dst_ch[pout], in_=ps, func=ACT.Identity,
                bias=bias_col[:, pout:pout + 1]),
                reads=[pres, r_const], writes=[dst_res[pout]])

    def favor(src_ch, src_res, om_ap):
        for p in range(NCH):
            q2a = q2
            q2res = r_q2[0]
            P.op("vector", lambda E, p=p, q2a=q2a: E.tensor_mul(
                out=q2a, in0=src_ch[p], in1=src_ch[p]),
                reads=[src_res[p]], writes=[q2res])
            psu, ures = next_ps()
            P.group("tensor", [
                lambda E, psu=psu, p=p, om_ap=om_ap: E.matmul(
                    out=psu, lhsT=om_ap, rhs=src_ch[p],
                    start=True, stop=False),
                lambda E, psu=psu, q2a=q2a: E.matmul(
                    out=psu, lhsT=onesblk, rhs=q2a, start=False, stop=True)],
                reads=[src_res[p], q2res, r_const], writes=[ures])
            P.op("scalar", lambda E, p=p, psu=psu: E.activation(
                out=src_ch[p], in_=psu, func=ACT.Exp),
                reads=[ures, r_const], writes=[src_res[p]])
            P.fill(0.10)

    def v_aug_ap(sub, h):
        base = sub * NH * 65 + h * 65
        return v_tm[:, base:base + 65]

    def emit_attn_layer(l, src_ch, src_res):
        om_ap = omblk[:, l * 128:(l + 1) * 128]
        P.op("gpsimd", lambda E, l=l: E.dma_start(
            out=lnrep.rearrange("p (j f) -> p j f", j=2), in_=ln_s[l, 0:2]
            .transpose([1, 0, 2])),
            reads=[r_ln_s[l][0], r_ln_s[l][1]], writes=[r_lnrep],
            dma="lnld")
        slot, slres = wload(wq_s[l], r_wq_s[l])
        proj_fm(qf_ch, r_qf, slot, slres, src_ch, src_res,
                bqk[:, l * 16:l * 16 + 8])
        slot, slres = wload(wk_s[l], r_wk_s[l])
        proj_fm(kf_ch, r_kf, slot, slres, src_ch, src_res,
                bqk[:, l * 16 + 8:l * 16 + 16])
        slot, slres = wload(wv_s[l], r_wv_s[l])
        bvr, bvres = load_row(bv_d[l:l + 1, :])
        for sub in range(NSUB):
            for half in range(2):
                ps, pres = next_ps()
                fns = []
                for dc in range(NCH):
                    fns.append(lambda E, ps=ps, slot=slot, dc=dc, sub=sub,
                               half=half:
                               E.matmul(out=ps,
                                        lhsT=src_ch[dc][:, sub * 128:
                                                        (sub + 1) * 128],
                                        rhs=slot[:, dc * D + half * 512:
                                                 dc * D + half * 512 + 512],
                                        start=(dc == 0), stop=False))
                fns.append(lambda E, ps=ps, half=half, bvr=bvr:
                           E.matmul(out=ps, lhsT=ones1,
                                    rhs=bvr[:, half * 512:half * 512 + 512],
                                    start=False, stop=True))
                P.group("tensor", fns,
                        reads=[slres, r_const, bvres] + src_res, writes=[pres])
                ov = v_tm[:, sub * NH * 65 + half * 8 * 65:
                          sub * NH * 65 + (half * 8 + 8) * 65]\
                    .rearrange("p (h c) -> p h c", h=8)[:, :, 0:64]
                P.op("vector", lambda E, ov=ov, ps=ps: E.tensor_copy(
                    out=ov, in_=ps.rearrange("p (h c) -> p h c", h=8)),
                    reads=[pres], writes=[r_vtm[sub][half]])
        if attn_stop == "qkv":
            return
        favor(qf_ch, r_qf, om_ap)
        favor(kf_ch, r_kf, om_ap)
        if attn_stop == "favor":
            return
        for c in range(NSUB):
            for p in range(NCH):
                if "kt_skip" in skip:
                    continue
                i = nxt("T", 2)
                ps = psT_banks[i]
                pres = r_psT[i]
                if "kt_oneslot" in skip:
                    i = 0
                    ps = psT_banks[0]
                    pres = r_psT[0]
                src_tp = identb if "kt_const" in skip \
                    else kf_ch[p][:, c * 128:(c + 1) * 128]
                P.op("tensor", lambda E, ps=ps, src_tp=src_tp: E.transpose(
                    out=ps, in_=src_tp,
                    identity=identb),
                    reads=[r_kf[p], r_const], writes=[pres])
                if "kt_evac" in skip:
                    continue
                P.op("vector", lambda E, ps=ps, c=c, p=p: E.tensor_copy(
                    out=kft[:, c * D + p * 128:c * D + (p + 1) * 128], in_=ps),
                    reads=[pres], writes=[r_kft[c]])
                P.fill(0.033)
        if attn_stop == "kft":
            return
        for c in range(NSUB):
            for p in range(NH // 2):
                P.op("vector", lambda E, p=p, l=l: E.tensor_copy(
                    out=saug[:, p * 66:p * 66 + 65],
                    in_=sst[:, (l * 8 + p) * 65:(l * 8 + p + 1) * 65]),
                    reads=[r_sst[2 * p], r_sst[2 * p + 1]], writes=[r_saug[p]])
            for h in range(NH):
                p, par = h // 2, h % 2
                r0 = 64 * par
                ig = nxt("G", 2)
                gps = psG_banks[ig]
                gres = r_psG[ig]
                P.op("tensor", lambda E, gps=gps, p=p, r0=r0, c=c: E.matmul(
                    out=gps,
                    lhsT=kf_ch[p][r0:r0 + 64, c * 128:(c + 1) * 128],
                    rhs=qf_ch[p][r0:r0 + 64, c * 128:(c + 1) * 128],
                    start=True, stop=True),
                    reads=[r_kf[p], r_qf[p]], writes=[gres])
                im = nxt("gmb", 2)
                gm = gmb[:, im * 128:(im + 1) * 128]
                gmres = r_gmb[im]
                P.op("vector", lambda E, gm=gm, gps=gps: E.tensor_mul(
                    out=gm, in0=gps, in1=umask),
                    reads=[gres, r_const], writes=[gmres])
                i1 = h % 2
                n1 = psN_banks[i1][:, 0:65]
                n1res = r_psN1[i1]
                P.op("tensor", lambda E, n1=n1, gm=gm, c=c, h=h: E.matmul(
                    out=n1, lhsT=gm, rhs=v_aug_ap(c, h),
                    start=True, stop=True),
                    reads=[gmres, r_vtm[c][h // 8]], writes=[n1res])
                n2 = psN_banks[i1][:, 80:145]
                n2res = r_psN2[i1]
                P.op("tensor", lambda E, n2=n2, p=p, r0=r0, c=c: E.matmul(
                    out=n2,
                    lhsT=qf_ch[p][r0:r0 + 64, c * 128:(c + 1) * 128],
                    rhs=saug[r0:r0 + 64, p * 66:p * 66 + 65],
                    start=True, stop=True),
                    reads=[r_qf[p], r_saug[p]], writes=[n2res])
                ind = nxt("ndb", 2)
                nd = ndb[:, ind * 66:ind * 66 + 65]
                ndres = r_ndb[ind]
                P.op("scalar", lambda E, nd=nd, n2=n2: E.copy(
                    out=nd, in_=n2),
                    reads=[n2res], writes=[ndres])
                P.op("vector", lambda E, nd=nd, n1=n1: E.tensor_add(
                    out=nd, in0=nd, in1=n1),
                    reads=[n1res, ndres], writes=[ndres])
                irc = nxt("rcb", 2)
                rc = rcb[:, irc * 2:irc * 2 + 2]
                rcres = r_rcb[irc]
                P.op("vector", lambda E, rc=rc, nd=nd: E.tensor_scalar_add(
                    out=rc[:, 0:1], in0=nd[:, 64:65], scalar1=64 * ATTN_EPS),
                    reads=[ndres], writes=[rcres])
                P.op("vector", lambda E, rc=rc: E.reciprocal(
                    out=rc[:, 1:2], in_=rc[:, 0:1]),
                    reads=[rcres], writes=[rcres])
                P.op("scalar", lambda E, rc=rc, nd=nd, c=c, h=h:
                     E.activation(
                         out=a_tm[:, c * D + h * 64:c * D + (h + 1) * 64],
                         in_=nd[:, 0:64], func=ACT.Identity,
                         scale=rc[:, 1:2]),
                     reads=[ndres, rcres], writes=[r_atm[c]])
                dsfull, dsres = next_ps()
                ds = dsfull[r0:r0 + 64, 0:65]
                P.op("tensor", lambda E, ds=ds, c=c, h=h: E.matmul(
                    out=ds,
                    lhsT=kft[:, c * D + h * 64:c * D + (h + 1) * 64],
                    rhs=v_aug_ap(c, h), start=True, stop=True),
                    reads=[r_kft[c], r_vtm[c][h // 8]], writes=[dsres])
                P.op("vector", lambda E, ds=ds, r0=r0, p=p, l=l, h=h:
                     E.tensor_add(
                         out=sst[r0:r0 + 64,
                                 (l * 8 + p) * 65:(l * 8 + p + 1) * 65],
                         in0=sst[r0:r0 + 64,
                                 (l * 8 + p) * 65:(l * 8 + p + 1) * 65],
                         in1=ds),
                     reads=[dsres, r_sst[h]], writes=[r_sst[h]])
                P.fill(0.10)
        if attn_stop == "core":
            return
        for c in range(NSUB):
            for p in range(NCH):
                i = nxt("T", 2)
                ps = psT_banks[i]
                pres = r_psT[i]
                P.op("tensor", lambda E, ps=ps, c=c, p=p: E.transpose(
                    out=ps, in_=a_tm[:, c * D + p * 128:c * D + (p + 1) * 128],
                    identity=identb),
                    reads=[r_atm[c], r_const], writes=[pres])
                P.op("vector", lambda E, ps=ps, c=c, p=p: E.tensor_copy(
                    out=afm_ch[p][:, c * 128:(c + 1) * 128], in_=ps),
                    reads=[pres], writes=[r_afm[p]])
                P.fill(0.033)
        if attn_stop == "atp":
            return
        slot, slres = wload(wo_s[l], r_wo_s[l])
        bor, bores = load_row(bo_d[l:l + 1, :])
        for sub in range(NSUB):
            for half in range(2):
                ps, pres = next_ps()
                fns = []
                for dc in range(NCH):
                    fns.append(lambda E, ps=ps, slot=slot, dc=dc, sub=sub,
                               half=half:
                               E.matmul(out=ps,
                                        lhsT=afm_ch[dc][:, sub * 128:
                                                        (sub + 1) * 128],
                                        rhs=slot[:, dc * D + half * 512:
                                                 dc * D + half * 512 + 512],
                                        start=(dc == 0), stop=False))
                fns.append(lambda E, ps=ps, half=half, bor=bor:
                           E.matmul(out=ps, lhsT=ones1,
                                    rhs=bor[:, half * 512:half * 512 + 512],
                                    start=False, stop=True))
                P.group("tensor", fns,
                        reads=[slres, r_const, bores] + r_afm, writes=[pres])
                P.op("vector", lambda E, ps=ps, sub=sub, half=half:
                     E.tensor_add(
                         out=stream[:, sub * D + half * 512:
                                    sub * D + half * 512 + 512],
                         in0=stream[:, sub * D + half * 512:
                                    sub * D + half * 512 + 512],
                         in1=ps),
                     reads=[pres, r_stream[sub]], writes=[r_stream[sub]])
        for sub in range(NSUB):
            ap = stream[:, sub * D:(sub + 1) * D]
            emit_ln(ap, ap, sub, r_stream[sub], r_stream[sub],
                    gamma=lnrep[:, 0:D], beta=lnrep[:, D:2 * D],
                    res_gb=r_lnrep)
        P.op("gpsimd", lambda E, l=l: E.dma_start(
            out=lnrep.rearrange("p (j f) -> p j f", j=2), in_=ln_s[l, 2:4]
            .transpose([1, 0, 2])),
            reads=[r_ln_s[l][2], r_ln_s[l][3]], writes=[r_lnrep],
            dma="lnld")
        if attn_stop == "ln1":
            return
        cast_transpose(stream, r_stream, sfm_ch, r_sfm, a_tm, r_atm)
        bfr, bfres = load_row(bf2_d[l:l + 1, :])
        for fb in range(4):
            slot1, sl1res = wload(w1_s[l, fb], r_w1_s[l][fb])
            for fc in range(NCH):
                ps, pres = next_ps()
                fns = []
                for dc in range(NCH):
                    fns.append(lambda E, ps=ps, slot1=slot1, dc=dc, fc=fc:
                               E.matmul(out=ps,
                                        lhsT=slot1[:, dc * D + fc * 128:
                                                   dc * D + fc * 128 + 128],
                                        rhs=sfm_ch[dc],
                                        start=(dc == 0), stop=(dc == NCH - 1)))
                P.group("tensor", fns, reads=[sl1res] + r_sfm, writes=[pres])
                P.op("scalar", lambda E, ps=ps, fc=fc, fb=fb, l=l:
                     E.activation(out=y1_ch[fc], in_=ps, func=ACT.Relu,
                                  bias=bf1r[:, l * 32 + fb * 8 + fc:
                                            l * 32 + fb * 8 + fc + 1]),
                     reads=[pres, r_const], writes=[r_y1[fc]])
            slot2, sl2res = wload(w2_s[l, fb], r_w2_s[l][fb])
            for sub in range(NSUB):
                for half in range(2):
                    ps, pres = next_ps()
                    fns = []
                    for fc in range(NCH):
                        fns.append(lambda E, ps=ps, slot2=slot2, fc=fc,
                                   sub=sub, half=half:
                                   E.matmul(out=ps,
                                            lhsT=y1_ch[fc][:, sub * 128:
                                                           (sub + 1) * 128],
                                            rhs=slot2[:, fc * D + half * 512:
                                                      fc * D + half * 512 + 512],
                                            start=(fc == 0), stop=False))
                    if fb == 0:
                        fns.append(lambda E, ps=ps, half=half, bfr=bfr:
                                   E.matmul(out=ps, lhsT=ones1,
                                            rhs=bfr[:, half * 512:
                                                    half * 512 + 512],
                                            start=False, stop=True))
                    else:
                        prev = fns.pop()
                        fns.append(lambda E, ps=ps, slot2=slot2, sub=sub,
                                   half=half:
                                   E.matmul(out=ps,
                                            lhsT=y1_ch[NCH - 1][:, sub * 128:
                                                                (sub + 1) * 128],
                                            rhs=slot2[:, (NCH - 1) * D +
                                                      half * 512:
                                                      (NCH - 1) * D +
                                                      half * 512 + 512],
                                            start=False, stop=True))
                    P.group("tensor", fns,
                            reads=[sl2res, r_const, bfres] + r_y1,
                            writes=[pres])
                    P.op("vector", lambda E, ps=ps, sub=sub, half=half:
                         E.tensor_add(
                             out=stream[:, sub * D + half * 512:
                                        sub * D + half * 512 + 512],
                             in0=stream[:, sub * D + half * 512:
                                        sub * D + half * 512 + 512],
                             in1=ps),
                         reads=[pres, r_stream[sub]], writes=[r_stream[sub]])
        if attn_stop == "ffn":
            return
        for sub in range(NSUB):
            ap = stream[:, sub * D:(sub + 1) * D]
            emit_ln(ap, ap, sub, r_stream[sub], r_stream[sub],
                    gamma=lnrep[:, 0:D], beta=lnrep[:, D:2 * D],
                    res_gb=r_lnrep)

    def emit_moe(t):
        for sub in range(NSUB):
            ig = nxt("G", 2)
            gps = psG_banks[ig][:, 0:8]
            gres = r_psG[ig]
            fns = []
            for p in range(NCH):
                fns.append(lambda E, gps=gps, p=p, sub=sub:
                           E.matmul(out=gps,
                                    lhsT=xnfm_ch[p][:, sub * 128:
                                                    (sub + 1) * 128],
                                    rhs=gwT[:, p * 8:(p + 1) * 8],
                                    start=(p == 0), stop=(p == NCH - 1)))
            P.group("tensor", fns, reads=[r_const] + r_xnfm, writes=[gres])
            st = stats[:, sub * 24:(sub + 1) * 24]
            sres = r_stats[sub]
            ip = nxt("prb", 2)
            pr = prb[:, ip * 8:(ip + 1) * 8]
            prres = r_prb[ip]
            P.op("vector", lambda E, gps=gps, pr=pr: E.tensor_copy(
                out=pr, in_=gps), reads=[gres], writes=[prres])
            P.op("vector", lambda E, pr=pr, st=st: E.reduce_max(
                out=st[:, 17:18], in_=pr, axis=AX.X),
                reads=[prres], writes=[sres])
            P.op("vector", lambda E, st=st: E.tensor_scalar_mul(
                out=st[:, 18:19], in0=st[:, 17:18], scalar1=-1.0),
                reads=[sres], writes=[sres])
            P.op("scalar", lambda E, pr=pr, st=st: E.activation(
                out=pr, in_=pr, func=ACT.Exp, bias=st[:, 18:19]),
                reads=[sres, prres], writes=[prres])
            P.op("vector", lambda E, pr=pr, st=st: E.reduce_sum(
                out=st[:, 19:20], in_=pr, axis=AX.X),
                reads=[prres], writes=[sres])
            P.op("vector", lambda E, st=st: E.reciprocal(
                out=st[:, 20:21], in_=st[:, 19:20]),
                reads=[sres], writes=[sres])
            P.op("vector", lambda E, pr=pr, st=st: E.tensor_scalar_mul(
                out=pr, in0=pr, scalar1=st[:, 20:21]),
                reads=[sres, prres], writes=[prres])
            P.op("vector", lambda E, pr=pr, st=st: E.reduce_max(
                out=st[:, 21:22], in_=pr, axis=AX.X),
                reads=[prres], writes=[sres])
            ip2 = nxt("prb", 2)
            t8 = prb[:, ip2 * 8:(ip2 + 1) * 8]
            t8res = r_prb[ip2]
            P.op("vector", lambda E, pr=pr, st=st, t8=t8: E.tensor_scalar(
                out=t8, in0=pr, scalar1=st[:, 21:22], scalar2=1e30,
                op0=OP.is_equal, op1=OP.mult),
                reads=[prres, sres], writes=[t8res])
            P.op("vector", lambda E, pr=pr, t8=t8: E.tensor_sub(
                out=t8, in0=pr, in1=t8),
                reads=[prres, t8res], writes=[t8res])
            P.op("vector", lambda E, t8=t8, st=st: E.reduce_max(
                out=st[:, 22:23], in_=t8, axis=AX.X),
                reads=[t8res], writes=[sres])
            P.op("vector", lambda E, pr=pr, st=st, t8=t8: E.tensor_scalar(
                out=t8, in0=pr, scalar1=st[:, 22:23], scalar2=None,
                op0=OP.is_ge),
                reads=[prres, sres], writes=[t8res])
            P.op("vector", lambda E, pr=pr, t8=t8, sub=sub: E.tensor_mul(
                out=wtok[:, sub * 8:(sub + 1) * 8], in0=pr, in1=t8),
                reads=[prres, t8res], writes=[r_wtok[sub]])
            ig2 = nxt("G", 2)
            tps = psG_banks[ig2][0:8, 0:128]
            tres = r_psG[ig2]
            P.op("tensor", lambda E, tps=tps, sub=sub: E.transpose(
                out=tps, in_=wtok[:, sub * 8:(sub + 1) * 8],
                identity=identf),
                reads=[r_wtok[sub], r_const], writes=[tres])
            P.op("vector", lambda E, tps=tps, sub=sub: E.tensor_copy(
                out=wT, in_=tps),
                reads=[tres], writes=[r_wT[sub]])
        for sub in range(NSUB):
            for half in range(2):
                ps, pres = next_ps()
                P.op("tensor", lambda E, ps=ps, sub=sub, half=half: E.matmul(
                    out=ps, lhsT=wT,
                    rhs=b2r[:, half * 512:half * 512 + 512],
                    start=True, stop=True),
                    reads=[r_wT[sub], r_const], writes=[pres])
                P.op("vector", lambda E, ps=ps, sub=sub, half=half:
                     E.tensor_copy(
                         out=y2acc[:, sub * D + half * 512:
                                   sub * D + half * 512 + 512], in_=ps),
                     reads=[pres, r_out], writes=[r_y2[sub][half]])
        P.filler = []
        for e in range(NE):
            for fb in range(4):
                P.filler.append(lambda e=e, fb=fb: _moe_unit(e, fb))

    def _moe_unit(e, fb):
        if True:
            if True:
                slot1, sl1res = wload(m1_s[e, fb], r_m1_s[e])
                for fc in range(NCH):
                    ps, pres = next_ps()
                    fns = []
                    for dc in range(NCH):
                        fns.append(lambda E, ps=ps, slot1=slot1, dc=dc, fc=fc:
                                   E.matmul(
                                       out=ps,
                                       lhsT=slot1[:, dc * D + fc * 128:
                                                  dc * D + fc * 128 + 128],
                                       rhs=xnfm_ch[dc],
                                       start=(dc == 0), stop=(dc == NCH - 1)))
                    P.group("tensor", fns, reads=[sl1res] + r_xnfm,
                            writes=[pres])
                    P.op("scalar", lambda E, ps=ps, e=e, fb=fb, fc=fc:
                         E.activation(out=y1_ch[fc], in_=ps,
                                      func=ACT.Gelu_apprx_tanh,
                                      bias=b1r[:, e * 32 + fb * 8 + fc:
                                               e * 32 + fb * 8 + fc + 1]),
                         reads=[pres, r_const], writes=[r_y1[fc]])
                slot2, sl2res = wload(m2_s[e, fb], r_m2_s[e])
                for sub in range(NSUB):
                    for half in range(2):
                        ps, pres = next_ps()
                        fns = []
                        for fc in range(NCH):
                            fns.append(lambda E, ps=ps, slot2=slot2, fc=fc,
                                       sub=sub, half=half:
                                       E.matmul(
                                           out=ps,
                                           lhsT=y1_ch[fc][:, sub * 128:
                                                          (sub + 1) * 128],
                                           rhs=slot2[:, fc * D + half * 512:
                                                     fc * D + half * 512 + 512],
                                           start=(fc == 0),
                                           stop=(fc == NCH - 1)))
                        P.group("tensor", fns, reads=[sl2res] + r_y1,
                                writes=[pres])
                        im = nxt("moeps", 2)
                        mp = moeps[:, im * 512:(im + 1) * 512]
                        mpres = r_moeps[im]
                        P.op("scalar", lambda E, ps=ps, mp=mp, sub=sub, e=e:
                             E.activation(out=mp, in_=ps, func=ACT.Identity,
                                          scale=wtok[:, sub * 8 + e:
                                                     sub * 8 + e + 1]),
                             reads=[pres, r_wtok[sub]], writes=[mpres])
                        P.op("gpsimd", lambda E, mp=mp, sub=sub, half=half:
                             E.tensor_add(
                                 out=y2acc[:, sub * D + half * 512:
                                           sub * D + half * 512 + 512],
                                 in0=y2acc[:, sub * D + half * 512:
                                           sub * D + half * 512 + 512],
                                 in1=mp),
                             reads=[mpres, r_y2[sub][half]],
                             writes=[r_y2[sub][half]])

    for t in range(NT if "main" not in skip else 0):
        xv = x_d[t * T:(t + 1) * T, :].rearrange("(s p) f -> p s f", p=128)
        P.op("gpsimd", lambda E, xv=xv: E.dma_start(
            out=stream.rearrange("p (s f) -> p s f", s=NSUB), in_=xv),
            writes=r_stream, dma="xld")
        for sub in range(NSUB):
            emit_ln(stream[:, sub * D:(sub + 1) * D],
                    stream[:, sub * D:(sub + 1) * D],
                    sub, r_stream[sub], r_stream[sub])
            cast_transpose_sub(stream, r_stream, xnfm_ch, r_xnfm, a_tm,
                               r_atm, sub)
        if with_moe:
            emit_moe(t)
        else:
            for sub in range(NSUB):
                for half in range(2):
                    P.op("vector", lambda E, sub=sub, half=half: E.memset(
                        y2acc[:, sub * D + half * 512:
                              sub * D + half * 512 + 512], 0.0),
                        reads=[r_out], writes=[r_y2[sub][half]])
        if with_attn:
            for l in range(L):
                src_ch, src_res = (xnfm_ch, r_xnfm) if l == 0 else (sfm_ch,
                                                                    r_sfm)
                if l > 0:
                    cast_transpose(stream, r_stream, sfm_ch, r_sfm, a_tm,
                                   r_atm)
                emit_attn_layer(l, src_ch, src_res)
        P.flush_filler()
        for sub in range(NSUB):
            if "fin" in skip:
                continue
            P.op("vector", lambda E, sub=sub: E.tensor_add(
                out=y2acc[:, sub * D:(sub + 1) * D],
                in0=y2acc[:, sub * D:(sub + 1) * D],
                in1=stream[:, sub * D:(sub + 1) * D]),
                reads=[r_stream[sub], r_y2[sub][0], r_y2[sub][1]],
                writes=[r_y2[sub][0], r_y2[sub][1]])
            for half in range(2):
                xrv = x_d[t * T + sub * 128:t * T + (sub + 1) * 128,
                          half * 512:half * 512 + 512]
                P.op("gpsimd", lambda E, xrv=xrv: E.dma_start(out=xr, in_=xrv),
                     writes=[r_xr], dma="xrld")
                P.op("vector", lambda E, sub=sub, half=half: E.tensor_add(
                    out=y2acc[:, sub * D + half * 512:
                              sub * D + half * 512 + 512],
                    in0=y2acc[:, sub * D + half * 512:
                              sub * D + half * 512 + 512],
                    in1=xr),
                    reads=[r_xr, r_y2[sub][half]],
                    writes=[r_y2[sub][half]])
        ov = out_d[t * T:(t + 1) * T, :].rearrange("(s p) f -> p s f", p=128)
        P.op("gpsimd", lambda E, ov=ov: E.dma_start(
            out=ov, in_=y2acc.rearrange("p (s f) -> p s f", s=NSUB)),
            reads=[r_y2[s][h] for s in range(NSUB) for h in range(2)],
            writes=[r_out], dma="outst")

    P.q["gpsimd"].append(
        lambda E, s=P.sem["outst"], v=P.cnt["outst"]: E.wait_ge(s, v))

    P.replay()
    return nc


_CACHE = {}


def _get_program(S_len, L, with_moe, detect_races=False):
    key = (S_len, L, with_moe, detect_races)
    if key not in _CACHE:
        _CACHE[key] = build_program(S_len, L, with_moe, detect_races)
    return _CACHE[key]


def make_inputs(ins, S_len=4096, L=3):
    import ml_dtypes
    f32 = lambda a: np.ascontiguousarray(np.asarray(a, dtype=np.float32))
    base = {
        "Wq": f32(ins["L0_attn_Wq"][:L]),
        "Wk": f32(ins["L0_attn_Wk"][:L]),
        "Wv": f32(ins["L0_attn_Wv"][:L]),
        "Wo": f32(ins["L0_attn_Wo"][:L]),
        "bq": f32(ins["L0_attn_bq"][:L]),
        "bk": f32(ins["L0_attn_bk"][:L]),
        "bv": f32(ins["L0_attn_bv"][:L]),
        "bo": f32(ins["L0_attn_bo"][:L]),
        "omega": f32(ins["L0_attn_omega"][:L]),
        "ln1_g": f32(ins["L0_attn_ln1_g"][:L]),
        "ln1_b": f32(ins["L0_attn_ln1_b"][:L]),
        "ln2_g": f32(ins["L0_attn_ln2_g"][:L]),
        "ln2_b": f32(ins["L0_attn_ln2_b"][:L]),
        "W1": f32(ins["L0_attn_W1"][:L]),
        "bf1": f32(ins["L0_attn_bf1"][:L]),
        "W2": f32(ins["L0_attn_W2"][:L]),
        "bf2": f32(ins["L0_attn_bf2"][:L]),
        "gate_w": f32(ins["L0_ffn_gate_w"]),
        "mw1": f32(ins["L0_ffn_w1"]),
        "mb1": f32(ins["L0_ffn_b1"]),
        "mw2": f32(ins["L0_ffn_w2"]),
        "mb2": f32(ins["L0_ffn_b2"]),
        "ident_bf": np.eye(128, dtype=ml_dtypes.bfloat16),
        "ident_f32": np.eye(128, dtype=np.float32),
        "umask": np.triu(np.ones((128, 128), dtype=np.float32)),
    }
    x = f32(ins["L0_attn_x"])
    B = x.shape[0]
    maps = [dict(base, x=np.ascontiguousarray(x[b, :S_len])) for b in range(B)]
    return maps


def kernel(**inputs):
    S_len, L = 4096, 3
    nc = _get_program(S_len, L, True)
    in_maps = make_inputs(inputs, S_len, L)
    res = run_bass_kernel_spmd(nc, in_maps, core_ids=list(range(8)))
    out = np.stack([np.asarray(r["out"], dtype=np.float32)
                    for r in res.results], axis=0)
    return out


# version 10
# speedup vs baseline: 1.0286x; 1.0286x over previous
import math

import numpy as np

import concourse.bass as bass
import concourse.mybir as mybir
from concourse.bass_utils import run_bass_kernel_spmd

F32 = mybir.dt.float32
BF16 = mybir.dt.bfloat16
AX = mybir.AxisListType
ACT = mybir.ActivationFunctionType
OP = mybir.AluOpType

ENGS = ("sync", "tensor", "vector", "scalar", "gpsimd")

D = 1024
NH = 16
DH = 64
M2 = 64
DFF = 4096
NE = 8
T = 512
C = 128
LN_EPS = 1e-5
ATTN_EPS = 1e-6


class Res:
    __slots__ = ("ws", "rs")

    def __init__(self):
        self.ws = {}
        self.rs = {}

    def rdeps(self):
        return list(self.ws.items())

    def wdeps(self):
        return list(self.ws.items()) + list(self.rs.items())

    def did_read(self, ev):
        e, v = ev
        if self.rs.get(e, 0) < v:
            self.rs[e] = v

    def did_write(self, ev):
        e, v = ev
        if self.ws.get(e, 0) < v:
            self.ws[e] = v


class Prog:
    def __init__(self, nc):
        self.nc = nc
        self.q = {e: [] for e in ENGS}
        self.sem = {e: nc.alloc_semaphore("prog_" + e) for e in ENGS}
        self.cnt = {e: 0 for e in ENGS}
        self.lastwait = {e: {} for e in ENGS}
        self.filler = []
        self.fill_credit = 0.0

    def fill(self, credit):
        if not self.filler:
            return
        self.fill_credit += credit
        while self.fill_credit >= 1.0 and self.filler:
            self.fill_credit -= 1.0
            self.filler.pop(0)()

    def flush_filler(self):
        while self.filler:
            self.filler.pop(0)()

    def stream(self, name):
        self.sem[name] = self.nc.alloc_semaphore("dma_" + name)
        self.cnt[name] = 0

    def _deps(self, eng, reads, writes):
        need = {}
        for r in reads:
            for e2, v in r.rdeps():
                if need.get(e2, 0) < v:
                    need[e2] = v
        for w in writes:
            for e2, v in w.wdeps():
                if need.get(e2, 0) < v:
                    need[e2] = v
        for e2, v in need.items():
            if eng == "tensor" and e2 == "tensor":
                continue
            if self.lastwait[eng].get(e2, 0) < v:
                self.lastwait[eng][e2] = v
                s = self.sem[e2]
                self.q[eng].append(
                    lambda E, s=s, v=v: E.wait_ge(s, v))

    def op(self, eng, fn, reads=(), writes=(), dma=None):
        self._deps(eng, reads, writes)
        key = eng if dma is None else dma
        if dma is not None and self.cnt.get(key, 0) > 0:
            if self.lastwait[eng].get(key, 0) < self.cnt[key]:
                self.lastwait[eng][key] = self.cnt[key]
                s0, v0 = self.sem[key], self.cnt[key]
                self.q[eng].append(
                    lambda E, s0=s0, v0=v0: E.wait_ge(s0, v0))
        k = 1 if dma is None else 16
        self.cnt[key] += k
        s = self.sem[key]
        self.q[eng].append(lambda E, fn=fn, s=s, k=k: fn(E).then_inc(s, k))
        ev = (key, self.cnt[key])
        for r in reads:
            r.did_read(ev)
        for w in writes:
            w.did_write(ev)
        return ev

    def group(self, eng, fns, reads=(), writes=()):
        self._deps(eng, reads, writes)
        self.cnt[eng] += 1
        s = self.sem[eng]
        for fn in fns[:-1]:
            self.q[eng].append(lambda E, fn=fn: fn(E))
        fn = fns[-1]
        self.q[eng].append(lambda E, fn=fn, s=s: fn(E).then_inc(s, 1))
        ev = (eng, self.cnt[eng])
        for r in reads:
            r.did_read(ev)
        for w in writes:
            w.did_write(ev)
        return ev

    def replay(self):
        nc = self.nc
        with nc.Block() as blk:
            @blk.sync
            def _(E):
                for fn in self.q["sync"]:
                    fn(E)

            @blk.tensor
            def _(E):
                for fn in self.q["tensor"]:
                    fn(E)

            @blk.vector
            def _(E):
                for fn in self.q["vector"]:
                    fn(E)

            @blk.scalar
            def _(E):
                for fn in self.q["scalar"]:
                    fn(E)

            @blk.gpsimd
            def _(E):
                for fn in self.q["gpsimd"]:
                    fn(E)


def build_program(S_len=4096, L=3, with_moe=True, detect_races=False,
                  with_attn=True, attn_stop=None, skip=()):
    assert S_len % T == 0
    NT = S_len // T
    NSUB = T // 128
    NCH = 8
    NSLOT = 4

    nc = bass.Bass("TRN2", target_bir_lowering=False,
                   detect_race_conditions=detect_races)

    d_in = {}

    def din(name, shape, dt=F32):
        d_in[name] = nc.dram_tensor(name, list(shape), dt,
                                    kind="ExternalInput").ap()
        return d_in[name]

    x_d = din("x", [S_len, D])
    wq_d = din("Wq", [L, D, D])
    wk_d = din("Wk", [L, D, D])
    wv_d = din("Wv", [L, D, D])
    wo_d = din("Wo", [L, D, D])
    bq_d = din("bq", [L, D])
    bk_d = din("bk", [L, D])
    bv_d = din("bv", [L, D])
    bo_d = din("bo", [L, D])
    om_d = din("omega", [L, DH, 32])
    l1g_d = din("ln1_g", [L, D])
    l1b_d = din("ln1_b", [L, D])
    l2g_d = din("ln2_g", [L, D])
    l2b_d = din("ln2_b", [L, D])
    w1_d = din("W1", [L, D, DFF])
    bf1_d = din("bf1", [L, DFF])
    w2_d = din("W2", [L, DFF, D])
    bf2_d = din("bf2", [L, D])
    gw_d = din("gate_w", [NE, D])
    m1_d = din("mw1", [DFF, NE, D])
    mb1_d = din("mb1", [NE, DFF])
    m2_d = din("mw2", [D, NE, DFF])
    mb2_d = din("mb2", [NE, D])
    idb_d = din("ident_bf", [128, 128], BF16)
    idf_d = din("ident_f32", [128, 128])
    um_d = din("umask", [128, 128])

    out_d = nc.dram_tensor("out", [S_len, D], F32, kind="ExternalOutput").ap()

    wq_s = nc.dram_tensor("wq_s", [L, NCH, 128, D], BF16, kind="Internal").ap()
    wk_s = nc.dram_tensor("wk_s", [L, NCH, 128, D], BF16, kind="Internal").ap()
    wv_s = nc.dram_tensor("wv_s", [L, NCH, 128, D], BF16, kind="Internal").ap()
    wo_s = nc.dram_tensor("wo_s", [L, NCH, 128, D], BF16, kind="Internal").ap()
    w1_s = nc.dram_tensor("w1_s", [L, 4, NCH, 128, D], BF16, kind="Internal").ap()
    w2_s = nc.dram_tensor("w2_s", [L, 4, NCH, 128, D], BF16, kind="Internal").ap()
    m1_s = nc.dram_tensor("m1_s", [NE, 4, NCH, 128, D], BF16, kind="Internal").ap()
    m2_s = nc.dram_tensor("m2_s", [NE, 4, NCH, 128, D], BF16, kind="Internal").ap()
    ln_s = nc.dram_tensor("ln_s", [L, 4, 128, D], F32, kind="Internal").ap()

    def sb(name, shape, dt):
        return nc.alloc_sbuf_tensor(name, list(shape), dt).ap()

    stream = sb("stream", [128, NSUB * D], F32)
    y2acc = sb("y2acc", [128, NSUB * D], F32)
    xn_fm = sb("xn_fm", [128, NCH * T], BF16)
    sfm = sb("sfm", [128, NCH * T], BF16)
    a_fm = sb("a_fm", [128, NCH * T], BF16)
    qf = sb("qf", [128, NCH * T], BF16)
    kf = sb("kf", [128, NCH * T], BF16)
    kft = sb("kft", [128, NSUB * D], BF16)
    a_tm = sb("a_tm", [128, NSUB * D], BF16)
    v_tm = sb("v_tm", [128, NSUB * NH * 65], BF16)
    y1 = sb("y1", [128, NCH * T], BF16)
    q2 = sb("q2", [128, T], BF16)
    s_sc = sb("s_sc", [128, T], F32)
    wslot = sb("wslot", [128, NSLOT * NCH * D], BF16)
    lnrep = sb("lnrep", [128, 2 * D], F32)
    xr = sb("xr", [128, 512], F32)
    sst = sb("sst", [128, L * NH // 2 * 65], F32)
    saug = sb("saug", [128, NH // 2 * 66], BF16)
    gmb = sb("gmb", [128, 2 * 128], BF16)
    ndb = sb("ndb", [128, 2 * 66], F32)
    rcb = sb("rcb", [128, 8], F32)
    stats = sb("stats", [128, NSUB * 24], F32)
    moeps = sb("moeps", [128, 2 * 512], F32)
    wtok = sb("wtok", [128, NSUB * 8], F32)
    prb = sb("prb", [128, 2 * 8], F32)
    wT = sb("wT", [8, 128], F32)
    b2r = sb("b2r", [8, D], F32)
    bqk = sb("bqk", [128, L * 16], F32)
    bf1r = sb("bf1r", [128, L * 32], F32)
    b1r = sb("b1r", [128, NE * 32], F32)
    rowb = [sb("rowb0", [1, D], F32)]
    ones1 = sb("ones1", [1, 128], F32)
    gwT = sb("gwT", [128, NCH * 8], BF16)
    gwT32 = sb("gwT32", [128, NCH * 8], F32)
    omblk = sb("omblk", [128, L * 128], BF16)
    onesblk = sb("onesblk", [128, 128], BF16)
    identb = sb("identb", [128, 128], BF16)
    identf = sb("identf", [128, 128], F32)
    umask = sb("umask_sb", [128, 128], F32)
    omsb = sb("omsb", [128, 32], F32)
    cns = sb("cns", [128, 8], F32)

    psA = nc.alloc_psum_tensor("psA", [128, 512], F32).ap()
    psB = nc.alloc_psum_tensor("psB", [128, 512], F32).ap()
    psT0 = nc.alloc_psum_tensor("psT0", [128, 128], BF16).ap()
    psT1 = nc.alloc_psum_tensor("psT1", [128, 128], BF16).ap()
    psG0 = nc.alloc_psum_tensor("psG0", [128, 128], F32).ap()
    psG1 = nc.alloc_psum_tensor("psG1", [128, 128], F32).ap()
    psN0 = nc.alloc_psum_tensor("psN0", [128, 160], F32).ap()
    psN1b = nc.alloc_psum_tensor("psN1b", [128, 160], F32).ap()
    psT_banks = [psT0, psT1]
    psG_banks = [psG0, psG1]
    psN_banks = [psN0, psN1b]

    P = Prog(nc)
    for _s in ("ws0", "ws1", "ws2", "ws3", "p0a", "p0b", "p0c", "p0o",
               "p0lno", "cload", "lnld", "rowld", "xld", "xrld", "outst"):
        P.stream(_s)

    r_x = Res()
    r_stream = [Res() for _ in range(NSUB)]
    r_y2 = [[Res() for _ in range(2)] for _ in range(NSUB)]
    r_xnfm = [Res() for _ in range(NCH)]
    r_sfm = [Res() for _ in range(NCH)]
    r_afm = [Res() for _ in range(NCH)]
    r_qf = [Res() for _ in range(NCH)]
    r_kf = [Res() for _ in range(NCH)]
    r_kft = [Res() for _ in range(NSUB)]
    r_atm = [Res() for _ in range(NSUB)]
    r_vtm = [[Res() for _ in range(2)] for _ in range(NSUB)]
    r_y1 = [Res() for _ in range(NCH)]
    r_q2 = [Res() for _ in range(2)]
    r_ssc = [Res() for _ in range(2)]
    r_slot = [Res() for _ in range(NSLOT)]
    r_lnrep = Res()
    r_xr = Res()
    r_sst = [Res() for _ in range(NH)]
    r_saug = [Res() for _ in range(NH // 2)]
    r_gmb = [Res() for _ in range(2)]
    r_ndb = [Res() for _ in range(2)]
    r_rcb = [Res() for _ in range(2)]
    r_stats = [Res() for _ in range(NSUB)]
    r_moeps = [Res(), Res()]
    r_wtok = [Res() for _ in range(NSUB)]
    r_prb = [Res() for _ in range(2)]
    _rwT = Res()
    r_wT = [_rwT for _ in range(NSUB)]
    r_psA = Res()
    r_psB = Res()
    r_psG = [Res() for _ in range(2)]
    r_psN1 = [Res() for _ in range(2)]
    r_psN2 = [Res() for _ in range(2)]
    r_psT = [Res() for _ in range(2)]
    r_const = Res()
    r_rowb = [Res()]
    rowb_i = [0]

    def load_row(src_row_ap):
        i = 0
        P.op("gpsimd", lambda E, i=i, a=src_row_ap: E.dma_start(
            out=rowb[i], in_=a), writes=[r_rowb[i]], dma="rowld")
        return rowb[i], r_rowb[i]
    r_out = Res()
    r_wq_s = [Res() for _ in range(L)]
    r_wk_s = [Res() for _ in range(L)]
    r_wv_s = [Res() for _ in range(L)]
    r_wo_s = [Res() for _ in range(L)]
    r_w1_s = [[Res() for _ in range(4)] for _ in range(L)]
    r_w2_s = [[Res() for _ in range(4)] for _ in range(L)]
    r_m1_s = [Res() for _ in range(NE)]
    r_m2_s = [Res() for _ in range(NE)]
    r_ln_s = [[Res() for _ in range(4)] for _ in range(L)]
    r_ph_f32 = [Res() for _ in range(3)]
    r_ph_bf = [Res() for _ in range(4)]

    big_ps = [(psA, r_psA), (psB, r_psB)]
    ps_i = [0]

    def next_ps(n=1):
        out = []
        for _ in range(n):
            out.append(big_ps[ps_i[0] % 2])
            ps_i[0] += 1
        return out if n > 1 else out[0]

    rot = {"G": [0], "N1": [0], "N2": [0], "T": [0], "q2": [0],
           "ssc": [0], "gmb": [0], "ndb": [0], "rcb": [0], "prb": [0],
           "moeps": [0]}

    def nxt(key, n):
        i = rot[key][0] % n
        rot[key][0] += 1
        return i

    ev0 = P.op("vector", lambda E: E.memset(ones1, 1.0), writes=[r_const])
    P.op("vector", lambda E: E.memset(cns[:, 0:1], 64.0), writes=[r_const])
    P.op("scalar", lambda E: E.activation(out=cns[:, 1:2], in_=cns[:, 0:1],
                                          func=ACT.Ln),
         reads=[r_const], writes=[r_const])
    P.op("vector", lambda E: E.tensor_scalar_mul(out=cns[:, 2:3],
                                                 in0=cns[:, 1:2], scalar1=-0.5),
         reads=[r_const], writes=[r_const])
    P.op("vector", lambda E: E.tensor_scalar_mul(out=cns[:, 3:4],
                                                 in0=cns[:, 1:2],
                                                 scalar1=-0.25),
         reads=[r_const], writes=[r_const])
    P.op("scalar", lambda E: E.activation(out=cns[:, 4:5], in_=cns[:, 3:4],
                                          func=ACT.Exp),
         reads=[r_const], writes=[r_const])
    P.op("gpsimd", lambda E: E.dma_start(out=identb, in_=idb_d),
         writes=[r_const], dma="cload")
    P.op("gpsimd", lambda E: E.dma_start(out=identf, in_=idf_d),
         writes=[r_const], dma="cload")
    P.op("gpsimd", lambda E: E.dma_start(out=umask, in_=um_d),
         writes=[r_const], dma="cload")
    for l in range(L):
        for (dd, base) in ((bq_d, 0), (bk_d, 8)):
            P.op("gpsimd", lambda E, dd=dd, l=l, base=base: E.dma_start(
                out=bqk[:, l * 16 + base: l * 16 + base + 8],
                in_=dd[l].rearrange("(c p) -> p c", p=128),
                allow_slow_non_contiguous=True),
                writes=[r_const], dma="cload")
        P.op("gpsimd", lambda E, l=l: E.dma_start(
            out=bf1r[:, l * 32:(l + 1) * 32],
            in_=bf1_d[l].rearrange("(c p) -> p c", p=128),
            allow_slow_non_contiguous=True),
            writes=[r_const], dma="cload")
    if with_moe:
        for e in range(NE):
            P.op("gpsimd", lambda E, e=e: E.dma_start(
                out=b1r[:, e * 32:(e + 1) * 32],
                in_=mb1_d[e].rearrange("(c p) -> p c", p=128),
                allow_slow_non_contiguous=True),
                writes=[r_const], dma="cload")
        P.op("gpsimd", lambda E: E.dma_start(out=b2r, in_=mb2_d),
             writes=[r_const], dma="cload")
        for c in range(NCH):
            P.op("gpsimd", lambda E, c=c: E.dma_start(
                out=gwT32[:, c * 8:(c + 1) * 8],
                in_=gw_d[:, c * 128:(c + 1) * 128].transpose([1, 0]),
                allow_slow_non_contiguous=True),
                writes=[r_const], dma="cload")
        P.op("vector", lambda E: E.tensor_copy(out=gwT, in_=gwT32),
             reads=[r_const], writes=[r_const])
    for l in range(L):
        P.op("gpsimd", lambda E, l=l: E.dma_start(out=omsb[0:64, :],
                                                  in_=om_d[l]),
             writes=[r_const], dma="cload")
        P.op("vector", lambda E: E.tensor_copy(out=omsb[64:128, :],
                                               in_=omsb[0:64, :]),
             reads=[r_const], writes=[r_const])
        stg = stream[:, 0:128]
        P.op("vector", lambda E, stg=stg: E.memset(stg, 0.0),
             writes=[r_ph_f32[1]])
        for (p0, p1, c0, neg) in ((0, 64, 0, False), (0, 64, 32, True),
                                  (64, 128, 64, False), (64, 128, 96, True)):
            sc = cns[:, 5:6] if neg else cns[:, 4:5]
            if neg:
                P.op("vector", lambda E: E.tensor_scalar_mul(
                    out=cns[:, 5:6], in0=cns[:, 4:5], scalar1=-1.0),
                    reads=[r_const], writes=[r_const])
            P.op("vector", lambda E, p0=p0, p1=p1, c0=c0, sc=sc, stg=stg:
                 E.tensor_scalar_mul(out=stg[p0:p1, c0:c0 + 32],
                                     in0=omsb[p0:p1, :], scalar1=sc[p0:p1, :]),
                 reads=[r_const], writes=[r_ph_f32[1]])
        P.op("vector", lambda E, l=l, stg=stg: E.tensor_copy(
            out=omblk[:, l * 128:(l + 1) * 128], in_=stg),
            reads=[r_ph_f32[1]], writes=[r_const])
    P.op("vector", lambda E: E.memset(onesblk, 0.0), writes=[r_const])
    P.op("vector", lambda E: E.memset(onesblk[0:64, 0:64], -0.0625),
         writes=[r_const])
    P.op("vector", lambda E: E.memset(onesblk[64:128, 64:128], -0.0625),
         writes=[r_const])
    for sub in range(NSUB):
        base = sub * NH * 65
        ap = v_tm[:, base:base + NH * 65].rearrange("p (h c) -> p h c", h=NH)
        P.op("vector", lambda E, ap=ap: E.memset(ap[:, :, 64:65], 1.0),
             writes=[r_vtm[sub][0], r_vtm[sub][1]])
    P.op("vector", lambda E: E.memset(sst, 0.0), writes=r_sst)

    do_cast = "cast" not in skip
    ph_f32 = [(stream, r_ph_f32[1], "p0b", "gpsimd"),
              (y2acc, r_ph_f32[2], "p0c", "sync")]
    ph_bf = [(sfm, r_ph_bf[0]), (a_fm, r_ph_bf[1]), (qf, r_ph_bf[2]),
             (kf, r_ph_bf[3])]
    ph_mc = [(y1, r_y1[0]), (kft, r_kft[0]), (a_tm, r_atm[0])]
    ph_i = [0]
    bf_i = [0]
    mc_i = [0]

    def f32buf():
        i = ph_i[0] % 2
        ph_i[0] += 1
        return ph_f32[i]

    def bfbuf():
        i = bf_i[0] % 4
        bf_i[0] += 1
        return ph_bf[i]

    def mcbuf():
        i = mc_i[0] % 3
        mc_i[0] += 1
        return ph_mc[i]

    def cast_block(in_view, out_view, ncols, scr_res):
        fb, fres, fstr, fq = f32buf()
        bb, bres = bfbuf()
        n = ncols * 1024
        fv = fb[:, 0:n].rearrange("p (c f) -> p c f", c=ncols)
        bv = bb[:, 0:n].rearrange("p (c f) -> p c f", c=ncols)
        P.op(fq, lambda E, fv=fv, in_view=in_view: E.dma_start(
            out=fv, in_=in_view), writes=[fres], dma=fstr)
        P.op("vector", lambda E, bb=bb, fb=fb, n=n: E.tensor_copy(
            out=bb[:, 0:n], in_=fb[:, 0:n]), reads=[fres], writes=[bres])
        P.op("scalar", lambda E, out_view=out_view, bv=bv: E.dma_start(
            out=out_view, in_=bv), reads=[bres], writes=[scr_res],
            dma="p0o")

    def conv_attn_layer(l):
        for (src_, scr, rres) in ((wq_d, wq_s, r_wq_s), (wk_d, wk_s, r_wk_s),
                                  (wv_d, wv_s, r_wv_s), (wo_d, wo_s, r_wo_s)):
            for dc in range(0, NCH, 4):
                cast_block(
                    src_[l, dc * 128:(dc + 4) * 128, :]
                    .rearrange("(c p) f -> p c f", c=4),
                    scr[l, dc:dc + 4].transpose([1, 0, 2]),
                    4, rres[l])
        for dc in range(NCH):
            cast_block(
                w1_d[l, dc * 128:(dc + 1) * 128, :]
                .rearrange("p (c f) -> p c f", c=4),
                w1_s[l, :, dc].transpose([1, 0, 2]),
                4, r_w1_s[l][0])
        for j in range(NCH):
            cast_block(
                w2_d[l, j * 512:(j + 1) * 512, :]
                .rearrange("(c p) f -> p c f", c=4),
                w2_s[l].rearrange("a b p f -> (a b) p f")[j * 4:(j + 1) * 4]
                .transpose([1, 0, 2]),
                4, r_w2_s[l][0])

    tp_i = [0]

    def tp_slot():
        i = tp_i[0] % 2
        tp_i[0] += 1
        return psT_banks[i], r_psT[i]

    def conv_moe_expert(e):
        for fc in range(DFF // 128):
            fb_, fres, fstr, fq = f32buf()
            bb_, bres = bfbuf()
            mb_, mres = mcbuf()
            P.op(fq, lambda E, fc=fc, e=e, fb_=fb_: E.dma_start(
                out=fb_[:, 0:D], in_=m1_d[fc * 128:(fc + 1) * 128, e, :]),
                writes=[fres], dma=fstr)
            P.op("vector", lambda E, fb_=fb_, mb_=mb_: E.tensor_copy(
                out=mb_[:, 0:D], in_=fb_[:, 0:D]),
                reads=[fres], writes=[mres])
            for dc in range(NCH):
                ps, pres = tp_slot()
                P.op("tensor", lambda E, ps=ps, dc=dc, mb_=mb_:
                     E.transpose(out=ps,
                                 in_=mb_[:, dc * 128:(dc + 1) * 128],
                                 identity=identb),
                     reads=[mres, r_const], writes=[pres])
                P.op("vector", lambda E, bb_=bb_, dc=dc, ps=ps:
                     E.tensor_copy(out=bb_[:, dc * 128:(dc + 1) * 128],
                                   in_=ps),
                     reads=[pres], writes=[bres])
            ov = m1_s[e, fc // 8, :, :, (fc % 8) * 128:(fc % 8 + 1) * 128]\
                .transpose([1, 0, 2])
            iv = bb_[:, 0:1024].rearrange("p (c f) -> p c f", c=8)
            P.op("scalar", lambda E, ov=ov, iv=iv: E.dma_start(
                out=ov, in_=iv), reads=[bres], writes=[r_m1_s[e]],
                dma="p0o")
        for dc in range(NCH):
            fb_, fres, fstr, fq = f32buf()
            bb_, bres = bfbuf()
            mb_, mres = mcbuf()
            P.op(fq, lambda E, dc=dc, e=e, fb_=fb_: E.dma_start(
                out=fb_[:, 0:DFF],
                in_=m2_d[dc * 128:(dc + 1) * 128, e, :]),
                writes=[fres], dma=fstr)
            P.op("vector", lambda E, fb_=fb_, mb_=mb_: E.tensor_copy(
                out=mb_[:, 0:DFF], in_=fb_[:, 0:DFF]),
                reads=[fres], writes=[mres])
            for fc in range(DFF // 128):
                ps, pres = tp_slot()
                P.op("tensor", lambda E, ps=ps, fc=fc, mb_=mb_:
                     E.transpose(out=ps,
                                 in_=mb_[:, fc * 128:(fc + 1) * 128],
                                 identity=identb),
                     reads=[mres, r_const], writes=[pres])
                P.op("vector", lambda E, bb_=bb_, fc=fc, ps=ps:
                     E.tensor_copy(out=bb_[:, fc * 128:(fc + 1) * 128],
                                   in_=ps),
                     reads=[pres], writes=[bres])
            ov = m2_s[e, :, :, :, dc * 128:(dc + 1) * 128]\
                .transpose([2, 0, 1, 3])
            iv = bb_[:, 0:DFF].rearrange("p (a b f) -> p a b f", a=4, b=8)
            P.op("scalar", lambda E, ov=ov, iv=iv: E.dma_start(
                out=ov, in_=iv), reads=[bres], writes=[r_m2_s[e]],
                dma="p0o")

    lnrow = rowb[0]
    r_lnrow = r_rowb[0]
    for l in range(L if "lnrep" not in skip else 0):
        for (j, dd) in ((0, l1g_d), (1, l1b_d), (2, l2g_d), (3, l2b_d)):
            P.op("gpsimd", lambda E, dd=dd, l=l: E.dma_start(
                out=lnrow, in_=dd[l:l + 1, :]),
                writes=[r_lnrow], dma="rowld")
            fb_, fres, fstr, fq = f32buf()
            for h in range(2):
                P.op("tensor", lambda E, h=h: E.matmul(
                    out=(psA if h == 0 else psB)[:, 0:512], lhsT=ones1,
                    rhs=lnrow[:, h * 512:(h + 1) * 512],
                    start=True, stop=True),
                    reads=[r_lnrow, r_const],
                    writes=[r_psA if h == 0 else r_psB])
                P.op("vector", lambda E, h=h, fb_=fb_: E.tensor_copy(
                    out=fb_[:, h * 512:(h + 1) * 512],
                    in_=(psA if h == 0 else psB)[:, 0:512]),
                    reads=[r_psA if h == 0 else r_psB], writes=[fres])
            P.op("gpsimd", lambda E, l=l, j=j, fb_=fb_: E.dma_start(
                out=ln_s[l, j], in_=fb_[:, 0:D]),
                reads=[fres], writes=[r_ln_s[l][j]], dma="p0lno")

    conv_order = []
    for l in range(L if do_cast else 0):
        conv_order.append(("attn", l))
        if with_moe:
            if l == 0:
                conv_order += [("moe", 0), ("moe", 1)]
            elif l == 1:
                conv_order += [("moe", 2), ("moe", 3)]
    if with_moe:
        done = {e for k, e in conv_order if k == "moe"}
        conv_order += [("moe", e) for e in range(NE) if e not in done]
    for kind, idx in conv_order:
        if kind == "attn":
            conv_attn_layer(idx)
        else:
            conv_moe_expert(idx)
    for l in range(L):
        for fb in range(4):
            r_w1_s[l][fb] = r_w1_s[l][0]
            r_w2_s[l][fb] = r_w2_s[l][0]

    def _merge(phase, mains):
        for m in mains:
            for k, v in phase.ws.items():
                if m.rs.get(k, 0) < v:
                    m.rs[k] = v
                if m.ws.get(k, 0) < v:
                    m.ws[k] = v
            for k, v in phase.rs.items():
                if m.rs.get(k, 0) < v:
                    m.rs[k] = v
    _merge(r_ph_f32[1], r_stream)
    _merge(r_ph_f32[2], [r_y2[s][h] for s in range(NSUB) for h in range(2)])
    _merge(r_ph_bf[0], r_sfm)
    _merge(r_ph_bf[1], r_afm)
    _merge(r_ph_bf[2], r_qf)
    _merge(r_ph_bf[3], r_kf)
    _merge(r_y1[0], r_y1[1:])
    _merge(r_kft[0], r_kft[1:])
    _merge(r_atm[0], r_atm[1:])

    slot_i = [0]

    def wload(scr_view, scr_res):
        i = slot_i[0] % NSLOT
        slot_i[0] += 1
        sl = wslot[:, i * NCH * D:(i + 1) * NCH * D]
        res = r_slot[i]
        ov = sl.rearrange("p (c f) -> p c f", c=NCH)
        iv = scr_view.transpose([1, 0, 2])
        P.op("sync", lambda E, ov=ov, iv=iv: E.dma_start(
            out=ov[:, 0:NCH // 2, :], in_=iv[:, 0:NCH // 2, :]),
            reads=[scr_res], writes=[res], dma="ws%d" % i)
        P.op("sync", lambda E, ov=ov, iv=iv: E.dma_start(
            out=ov[:, NCH // 2:, :], in_=iv[:, NCH // 2:, :]),
            reads=[scr_res], writes=[res], dma="ws%d" % i)
        return sl, res

    def emit_ln(ap_in, ap_out, sub, res_in, res_out, gamma=None, beta=None,
               res_gb=None):
        st = stats[:, sub * 24:(sub + 1) * 24]
        sres = r_stats[sub]
        P.op("vector", lambda E: E.bn_stats(out=st[:, 0:6], in_=ap_in[:, 0:512]),
             reads=[res_in], writes=[sres])
        P.op("vector", lambda E: E.bn_stats(out=st[:, 6:12],
                                            in_=ap_in[:, 512:1024]),
             reads=[res_in], writes=[sres])
        P.op("vector", lambda E: E.bn_aggr(out=st[:, 12:14], in_=st[:, 0:12]),
             reads=[sres], writes=[sres])
        P.op("vector", lambda E: E.tensor_scalar_add(out=st[:, 14:15],
                                                     in0=st[:, 13:14],
                                                     scalar1=LN_EPS),
             reads=[sres], writes=[sres])
        P.op("vector", lambda E: E.reciprocal(out=st[:, 15:16],
                                              in_=st[:, 14:15]),
             reads=[sres], writes=[sres])
        P.op("scalar", lambda E: E.activation(out=st[:, 16:17],
                                              in_=st[:, 15:16], func=ACT.Sqrt),
             reads=[sres], writes=[sres])
        P.op("vector", lambda E: E.tensor_scalar(
            out=ap_out, in0=ap_in, scalar1=st[:, 12:13],
            scalar2=st[:, 16:17], op0=OP.subtract, op1=OP.mult),
            reads=[res_in, sres], writes=[res_out])
        if gamma is not None:
            P.op("gpsimd", lambda E: E.tensor_mul(out=ap_out, in0=ap_out,
                                                  in1=gamma),
                 reads=[res_gb, res_out], writes=[res_out])
            P.op("gpsimd", lambda E: E.tensor_add(out=ap_out, in0=ap_out,
                                                  in1=beta),
                 reads=[res_gb, res_out], writes=[res_out])
        P.fill(0.16)

    def cast_transpose(src_ap, src_res_list, dst_ap_chunks, dst_res,
                       scratch_ap, scratch_res):
        for sub in range(NSUB):
            cast_transpose_sub(src_ap, src_res_list, dst_ap_chunks, dst_res,
                               scratch_ap, scratch_res, sub)

    def cast_transpose_sub(src_ap, src_res_list, dst_ap_chunks, dst_res,
                           scratch_ap, scratch_res, sub):
        if True:
            P.op("vector", lambda E, sub=sub: E.tensor_copy(
                out=scratch_ap[:, sub * D:(sub + 1) * D],
                in_=src_ap[:, sub * D:(sub + 1) * D]),
                reads=[src_res_list[sub]], writes=[scratch_res[sub]])
            if "ctp_cast" in skip:
                return
            for p in range(NCH):
                i = nxt("T", 2)
                ps = psT_banks[i]
                pres = r_psT[i]
                P.op("tensor", lambda E, ps=ps, sub=sub, p=p: E.transpose(
                    out=ps,
                    in_=scratch_ap[:, sub * D + p * 128: sub * D + (p + 1) * 128],
                    identity=identb),
                    reads=[scratch_res[sub], r_const], writes=[pres])
                if "ctp_evac" in skip:
                    continue
                P.op("vector", lambda E, ps=ps, sub=sub, p=p: E.tensor_copy(
                    out=dst_ap_chunks[p][:, sub * 128:(sub + 1) * 128],
                    in_=ps),
                    reads=[pres], writes=[dst_res[p]])
                P.fill(0.033)

    def chunks_of(buf, n, w):
        return [buf[:, i * w:(i + 1) * w] for i in range(n)]

    xnfm_ch = chunks_of(xn_fm, NCH, T)
    sfm_ch = chunks_of(sfm, NCH, T)
    afm_ch = chunks_of(a_fm, NCH, T)
    qf_ch = chunks_of(qf, NCH, T)
    kf_ch = chunks_of(kf, NCH, T)
    y1_ch = chunks_of(y1, NCH, T)

    def proj_fm(dst_ch, dst_res, slot, slot_res, src_ch, src_res, bias_col):
        for pout in range(NCH):
            ps, pres = next_ps()
            fns = []
            for dc in range(NCH):
                fns.append(lambda E, ps=ps, slot=slot, dc=dc, pout=pout:
                           E.matmul(out=ps, lhsT=slot[:, dc * D + pout * 128:
                                                      dc * D + pout * 128 + 128],
                                    rhs=src_ch[dc],
                                    start=(dc == 0), stop=(dc == NCH - 1)))
            P.group("tensor", fns,
                    reads=[slot_res, r_const] + src_res, writes=[pres])
            P.op("scalar", lambda E, ps=ps, pout=pout: E.activation(
                out=dst_ch[pout], in_=ps, func=ACT.Identity,
                bias=bias_col[:, pout:pout + 1]),
                reads=[pres, r_const], writes=[dst_res[pout]])

    def favor(src_ch, src_res, om_ap):
        for p in range(NCH):
            q2a = q2
            q2res = r_q2[0]
            P.op("vector", lambda E, p=p, q2a=q2a: E.tensor_mul(
                out=q2a, in0=src_ch[p], in1=src_ch[p]),
                reads=[src_res[p]], writes=[q2res])
            psu, ures = next_ps()
            P.group("tensor", [
                lambda E, psu=psu, p=p, om_ap=om_ap: E.matmul(
                    out=psu, lhsT=om_ap, rhs=src_ch[p],
                    start=True, stop=False),
                lambda E, psu=psu, q2a=q2a: E.matmul(
                    out=psu, lhsT=onesblk, rhs=q2a, start=False, stop=True)],
                reads=[src_res[p], q2res, r_const], writes=[ures])
            P.op("scalar", lambda E, p=p, psu=psu: E.activation(
                out=src_ch[p], in_=psu, func=ACT.Exp),
                reads=[ures, r_const], writes=[src_res[p]])
            P.fill(0.10)

    def v_aug_ap(sub, h):
        base = sub * NH * 65 + h * 65
        return v_tm[:, base:base + 65]

    def emit_attn_layer(l, src_ch, src_res):
        om_ap = omblk[:, l * 128:(l + 1) * 128]
        P.op("gpsimd", lambda E, l=l: E.dma_start(
            out=lnrep.rearrange("p (j f) -> p j f", j=2), in_=ln_s[l, 0:2]
            .transpose([1, 0, 2])),
            reads=[r_ln_s[l][0], r_ln_s[l][1]], writes=[r_lnrep],
            dma="lnld")
        slot, slres = wload(wq_s[l], r_wq_s[l])
        proj_fm(qf_ch, r_qf, slot, slres, src_ch, src_res,
                bqk[:, l * 16:l * 16 + 8])
        slot, slres = wload(wk_s[l], r_wk_s[l])
        proj_fm(kf_ch, r_kf, slot, slres, src_ch, src_res,
                bqk[:, l * 16 + 8:l * 16 + 16])
        slot, slres = wload(wv_s[l], r_wv_s[l])
        bvr, bvres = load_row(bv_d[l:l + 1, :])
        for sub in range(NSUB):
            for half in range(2):
                ps, pres = next_ps()
                fns = []
                for dc in range(NCH):
                    fns.append(lambda E, ps=ps, slot=slot, dc=dc, sub=sub,
                               half=half:
                               E.matmul(out=ps,
                                        lhsT=src_ch[dc][:, sub * 128:
                                                        (sub + 1) * 128],
                                        rhs=slot[:, dc * D + half * 512:
                                                 dc * D + half * 512 + 512],
                                        start=(dc == 0), stop=False))
                fns.append(lambda E, ps=ps, half=half, bvr=bvr:
                           E.matmul(out=ps, lhsT=ones1,
                                    rhs=bvr[:, half * 512:half * 512 + 512],
                                    start=False, stop=True))
                P.group("tensor", fns,
                        reads=[slres, r_const, bvres] + src_res, writes=[pres])
                ov = v_tm[:, sub * NH * 65 + half * 8 * 65:
                          sub * NH * 65 + (half * 8 + 8) * 65]\
                    .rearrange("p (h c) -> p h c", h=8)[:, :, 0:64]
                P.op("vector", lambda E, ov=ov, ps=ps: E.tensor_copy(
                    out=ov, in_=ps.rearrange("p (h c) -> p h c", h=8)),
                    reads=[pres], writes=[r_vtm[sub][half]])
        if attn_stop == "qkv":
            return
        favor(qf_ch, r_qf, om_ap)
        favor(kf_ch, r_kf, om_ap)
        if attn_stop == "favor":
            return
        for c in range(NSUB):
            for p in range(NCH):
                if "kt_skip" in skip:
                    continue
                i = nxt("T", 2)
                ps = psT_banks[i]
                pres = r_psT[i]
                if "kt_oneslot" in skip:
                    i = 0
                    ps = psT_banks[0]
                    pres = r_psT[0]
                src_tp = identb if "kt_const" in skip \
                    else kf_ch[p][:, c * 128:(c + 1) * 128]
                P.op("tensor", lambda E, ps=ps, src_tp=src_tp: E.transpose(
                    out=ps, in_=src_tp,
                    identity=identb),
                    reads=[r_kf[p], r_const], writes=[pres])
                if "kt_evac" in skip:
                    continue
                P.op("vector", lambda E, ps=ps, c=c, p=p: E.tensor_copy(
                    out=kft[:, c * D + p * 128:c * D + (p + 1) * 128], in_=ps),
                    reads=[pres], writes=[r_kft[c]])
                P.fill(0.033)
        if attn_stop == "kft":
            return
        for c in range(NSUB):
            for p in range(NH // 2):
                P.op("vector", lambda E, p=p, l=l: E.tensor_copy(
                    out=saug[:, p * 66:p * 66 + 65],
                    in_=sst[:, (l * 8 + p) * 65:(l * 8 + p + 1) * 65]),
                    reads=[r_sst[2 * p], r_sst[2 * p + 1]], writes=[r_saug[p]])
            def emit_g(h, c=c):
                p, par = h // 2, h % 2
                r0 = 64 * par
                ig = nxt("G", 2)
                gps = psG_banks[ig]
                gres = r_psG[ig]
                P.op("tensor", lambda E, gps=gps, p=p, r0=r0, c=c: E.matmul(
                    out=gps,
                    lhsT=kf_ch[p][r0:r0 + 64, c * 128:(c + 1) * 128],
                    rhs=qf_ch[p][r0:r0 + 64, c * 128:(c + 1) * 128],
                    start=True, stop=True),
                    reads=[r_kf[p], r_qf[p]], writes=[gres])
                return gps, gres

            for h in range(NH):
                p, par = h // 2, h % 2
                r0 = 64 * par
                gps, gres = emit_g(h)
                im = nxt("gmb", 2)
                gm = gmb[:, im * 128:(im + 1) * 128]
                gmres = r_gmb[im]
                P.op("vector", lambda E, gm=gm, gps=gps: E.tensor_mul(
                    out=gm, in0=gps, in1=umask),
                    reads=[gres, r_const], writes=[gmres])
                i1 = h % 2
                n1 = psN_banks[i1][:, 0:65]
                n1res = r_psN1[i1]
                P.op("tensor", lambda E, n1=n1, gm=gm, c=c, h=h: E.matmul(
                    out=n1, lhsT=gm, rhs=v_aug_ap(c, h),
                    start=True, stop=True),
                    reads=[gmres, r_vtm[c][h // 8]], writes=[n1res])
                n2 = psN_banks[i1][:, 80:145]
                n2res = r_psN2[i1]
                P.op("tensor", lambda E, n2=n2, p=p, r0=r0, c=c: E.matmul(
                    out=n2,
                    lhsT=qf_ch[p][r0:r0 + 64, c * 128:(c + 1) * 128],
                    rhs=saug[r0:r0 + 64, p * 66:p * 66 + 65],
                    start=True, stop=True),
                    reads=[r_qf[p], r_saug[p]], writes=[n2res])
                ind = nxt("ndb", 2)
                nd = ndb[:, ind * 66:ind * 66 + 65]
                ndres = r_ndb[ind]
                P.op("scalar", lambda E, nd=nd, n2=n2: E.copy(
                    out=nd, in_=n2),
                    reads=[n2res], writes=[ndres])
                P.op("vector", lambda E, nd=nd, n1=n1: E.tensor_add(
                    out=nd, in0=nd, in1=n1),
                    reads=[n1res, ndres], writes=[ndres])
                irc = nxt("rcb", 2)
                rc = rcb[:, irc * 2:irc * 2 + 2]
                rcres = r_rcb[irc]
                P.op("vector", lambda E, rc=rc, nd=nd: E.tensor_scalar_add(
                    out=rc[:, 0:1], in0=nd[:, 64:65], scalar1=64 * ATTN_EPS),
                    reads=[ndres], writes=[rcres])
                P.op("vector", lambda E, rc=rc: E.reciprocal(
                    out=rc[:, 1:2], in_=rc[:, 0:1]),
                    reads=[rcres], writes=[rcres])
                P.op("scalar", lambda E, rc=rc, nd=nd, c=c, h=h:
                     E.activation(
                         out=a_tm[:, c * D + h * 64:c * D + (h + 1) * 64],
                         in_=nd[:, 0:64], func=ACT.Identity,
                         scale=rc[:, 1:2]),
                     reads=[ndres, rcres], writes=[r_atm[c]])
                dsfull, dsres = next_ps()
                ds = dsfull[r0:r0 + 64, 0:65]
                P.op("tensor", lambda E, ds=ds, c=c, h=h: E.matmul(
                    out=ds,
                    lhsT=kft[:, c * D + h * 64:c * D + (h + 1) * 64],
                    rhs=v_aug_ap(c, h), start=True, stop=True),
                    reads=[r_kft[c], r_vtm[c][h // 8]], writes=[dsres])
                P.op("vector", lambda E, ds=ds, r0=r0, p=p, l=l, h=h:
                     E.tensor_add(
                         out=sst[r0:r0 + 64,
                                 (l * 8 + p) * 65:(l * 8 + p + 1) * 65],
                         in0=sst[r0:r0 + 64,
                                 (l * 8 + p) * 65:(l * 8 + p + 1) * 65],
                         in1=ds),
                     reads=[dsres, r_sst[h]], writes=[r_sst[h]])
                P.fill(0.10)
        if attn_stop == "core":
            return
        for c in range(NSUB):
            for p in range(NCH):
                i = nxt("T", 2)
                ps = psT_banks[i]
                pres = r_psT[i]
                P.op("tensor", lambda E, ps=ps, c=c, p=p: E.transpose(
                    out=ps, in_=a_tm[:, c * D + p * 128:c * D + (p + 1) * 128],
                    identity=identb),
                    reads=[r_atm[c], r_const], writes=[pres])
                P.op("vector", lambda E, ps=ps, c=c, p=p: E.tensor_copy(
                    out=afm_ch[p][:, c * 128:(c + 1) * 128], in_=ps),
                    reads=[pres], writes=[r_afm[p]])
                P.fill(0.033)
        if attn_stop == "atp":
            return
        slot, slres = wload(wo_s[l], r_wo_s[l])
        bor, bores = load_row(bo_d[l:l + 1, :])
        for sub in range(NSUB):
            for half in range(2):
                ps, pres = next_ps()
                fns = []
                for dc in range(NCH):
                    fns.append(lambda E, ps=ps, slot=slot, dc=dc, sub=sub,
                               half=half:
                               E.matmul(out=ps,
                                        lhsT=afm_ch[dc][:, sub * 128:
                                                        (sub + 1) * 128],
                                        rhs=slot[:, dc * D + half * 512:
                                                 dc * D + half * 512 + 512],
                                        start=(dc == 0), stop=False))
                fns.append(lambda E, ps=ps, half=half, bor=bor:
                           E.matmul(out=ps, lhsT=ones1,
                                    rhs=bor[:, half * 512:half * 512 + 512],
                                    start=False, stop=True))
                P.group("tensor", fns,
                        reads=[slres, r_const, bores] + r_afm, writes=[pres])
                P.op("vector", lambda E, ps=ps, sub=sub, half=half:
                     E.tensor_add(
                         out=stream[:, sub * D + half * 512:
                                    sub * D + half * 512 + 512],
                         in0=stream[:, sub * D + half * 512:
                                    sub * D + half * 512 + 512],
                         in1=ps),
                     reads=[pres, r_stream[sub]], writes=[r_stream[sub]])
        for sub in range(NSUB):
            ap = stream[:, sub * D:(sub + 1) * D]
            emit_ln(ap, ap, sub, r_stream[sub], r_stream[sub],
                    gamma=lnrep[:, 0:D], beta=lnrep[:, D:2 * D],
                    res_gb=r_lnrep)
        P.op("gpsimd", lambda E, l=l: E.dma_start(
            out=lnrep.rearrange("p (j f) -> p j f", j=2), in_=ln_s[l, 2:4]
            .transpose([1, 0, 2])),
            reads=[r_ln_s[l][2], r_ln_s[l][3]], writes=[r_lnrep],
            dma="lnld")
        if attn_stop == "ln1":
            return
        cast_transpose(stream, r_stream, sfm_ch, r_sfm, a_tm, r_atm)
        bfr, bfres = load_row(bf2_d[l:l + 1, :])
        for fb in range(4):
            slot1, sl1res = wload(w1_s[l, fb], r_w1_s[l][fb])
            for fc in range(NCH):
                ps, pres = next_ps()
                fns = []
                for dc in range(NCH):
                    fns.append(lambda E, ps=ps, slot1=slot1, dc=dc, fc=fc:
                               E.matmul(out=ps,
                                        lhsT=slot1[:, dc * D + fc * 128:
                                                   dc * D + fc * 128 + 128],
                                        rhs=sfm_ch[dc],
                                        start=(dc == 0), stop=(dc == NCH - 1)))
                P.group("tensor", fns, reads=[sl1res] + r_sfm, writes=[pres])
                P.op("scalar", lambda E, ps=ps, fc=fc, fb=fb, l=l:
                     E.activation(out=y1_ch[fc], in_=ps, func=ACT.Relu,
                                  bias=bf1r[:, l * 32 + fb * 8 + fc:
                                            l * 32 + fb * 8 + fc + 1]),
                     reads=[pres, r_const], writes=[r_y1[fc]])
            slot2, sl2res = wload(w2_s[l, fb], r_w2_s[l][fb])
            for sub in range(NSUB):
                for half in range(2):
                    ps, pres = next_ps()
                    fns = []
                    for fc in range(NCH):
                        fns.append(lambda E, ps=ps, slot2=slot2, fc=fc,
                                   sub=sub, half=half:
                                   E.matmul(out=ps,
                                            lhsT=y1_ch[fc][:, sub * 128:
                                                           (sub + 1) * 128],
                                            rhs=slot2[:, fc * D + half * 512:
                                                      fc * D + half * 512 + 512],
                                            start=(fc == 0), stop=False))
                    if fb == 0:
                        fns.append(lambda E, ps=ps, half=half, bfr=bfr:
                                   E.matmul(out=ps, lhsT=ones1,
                                            rhs=bfr[:, half * 512:
                                                    half * 512 + 512],
                                            start=False, stop=True))
                    else:
                        prev = fns.pop()
                        fns.append(lambda E, ps=ps, slot2=slot2, sub=sub,
                                   half=half:
                                   E.matmul(out=ps,
                                            lhsT=y1_ch[NCH - 1][:, sub * 128:
                                                                (sub + 1) * 128],
                                            rhs=slot2[:, (NCH - 1) * D +
                                                      half * 512:
                                                      (NCH - 1) * D +
                                                      half * 512 + 512],
                                            start=False, stop=True))
                    P.group("tensor", fns,
                            reads=[sl2res, r_const, bfres] + r_y1,
                            writes=[pres])
                    P.op("vector", lambda E, ps=ps, sub=sub, half=half:
                         E.tensor_add(
                             out=stream[:, sub * D + half * 512:
                                        sub * D + half * 512 + 512],
                             in0=stream[:, sub * D + half * 512:
                                        sub * D + half * 512 + 512],
                             in1=ps),
                         reads=[pres, r_stream[sub]], writes=[r_stream[sub]])
        if attn_stop == "ffn":
            return
        for sub in range(NSUB):
            ap = stream[:, sub * D:(sub + 1) * D]
            emit_ln(ap, ap, sub, r_stream[sub], r_stream[sub],
                    gamma=lnrep[:, 0:D], beta=lnrep[:, D:2 * D],
                    res_gb=r_lnrep)

    def emit_moe(t):
        for sub in range(NSUB):
            ig = nxt("G", 2)
            gps = psG_banks[ig][:, 0:8]
            gres = r_psG[ig]
            fns = []
            for p in range(NCH):
                fns.append(lambda E, gps=gps, p=p, sub=sub:
                           E.matmul(out=gps,
                                    lhsT=xnfm_ch[p][:, sub * 128:
                                                    (sub + 1) * 128],
                                    rhs=gwT[:, p * 8:(p + 1) * 8],
                                    start=(p == 0), stop=(p == NCH - 1)))
            P.group("tensor", fns, reads=[r_const] + r_xnfm, writes=[gres])
            st = stats[:, sub * 24:(sub + 1) * 24]
            sres = r_stats[sub]
            ip = nxt("prb", 2)
            pr = prb[:, ip * 8:(ip + 1) * 8]
            prres = r_prb[ip]
            P.op("vector", lambda E, gps=gps, pr=pr: E.tensor_copy(
                out=pr, in_=gps), reads=[gres], writes=[prres])
            P.op("vector", lambda E, pr=pr, st=st: E.reduce_max(
                out=st[:, 17:18], in_=pr, axis=AX.X),
                reads=[prres], writes=[sres])
            P.op("vector", lambda E, st=st: E.tensor_scalar_mul(
                out=st[:, 18:19], in0=st[:, 17:18], scalar1=-1.0),
                reads=[sres], writes=[sres])
            P.op("scalar", lambda E, pr=pr, st=st: E.activation(
                out=pr, in_=pr, func=ACT.Exp, bias=st[:, 18:19]),
                reads=[sres, prres], writes=[prres])
            P.op("vector", lambda E, pr=pr, st=st: E.reduce_sum(
                out=st[:, 19:20], in_=pr, axis=AX.X),
                reads=[prres], writes=[sres])
            P.op("vector", lambda E, st=st: E.reciprocal(
                out=st[:, 20:21], in_=st[:, 19:20]),
                reads=[sres], writes=[sres])
            P.op("vector", lambda E, pr=pr, st=st: E.tensor_scalar_mul(
                out=pr, in0=pr, scalar1=st[:, 20:21]),
                reads=[sres, prres], writes=[prres])
            P.op("vector", lambda E, pr=pr, st=st: E.reduce_max(
                out=st[:, 21:22], in_=pr, axis=AX.X),
                reads=[prres], writes=[sres])
            ip2 = nxt("prb", 2)
            t8 = prb[:, ip2 * 8:(ip2 + 1) * 8]
            t8res = r_prb[ip2]
            P.op("vector", lambda E, pr=pr, st=st, t8=t8: E.tensor_scalar(
                out=t8, in0=pr, scalar1=st[:, 21:22], scalar2=1e30,
                op0=OP.is_equal, op1=OP.mult),
                reads=[prres, sres], writes=[t8res])
            P.op("vector", lambda E, pr=pr, t8=t8: E.tensor_sub(
                out=t8, in0=pr, in1=t8),
                reads=[prres, t8res], writes=[t8res])
            P.op("vector", lambda E, t8=t8, st=st: E.reduce_max(
                out=st[:, 22:23], in_=t8, axis=AX.X),
                reads=[t8res], writes=[sres])
            P.op("vector", lambda E, pr=pr, st=st, t8=t8: E.tensor_scalar(
                out=t8, in0=pr, scalar1=st[:, 22:23], scalar2=None,
                op0=OP.is_ge),
                reads=[prres, sres], writes=[t8res])
            P.op("vector", lambda E, pr=pr, t8=t8, sub=sub: E.tensor_mul(
                out=wtok[:, sub * 8:(sub + 1) * 8], in0=pr, in1=t8),
                reads=[prres, t8res], writes=[r_wtok[sub]])
            ig2 = nxt("G", 2)
            tps = psG_banks[ig2][0:8, 0:128]
            tres = r_psG[ig2]
            P.op("tensor", lambda E, tps=tps, sub=sub: E.transpose(
                out=tps, in_=wtok[:, sub * 8:(sub + 1) * 8],
                identity=identf),
                reads=[r_wtok[sub], r_const], writes=[tres])
            P.op("vector", lambda E, tps=tps, sub=sub: E.tensor_copy(
                out=wT, in_=tps),
                reads=[tres], writes=[r_wT[sub]])
        for sub in range(NSUB):
            for half in range(2):
                ps, pres = next_ps()
                P.op("tensor", lambda E, ps=ps, sub=sub, half=half: E.matmul(
                    out=ps, lhsT=wT,
                    rhs=b2r[:, half * 512:half * 512 + 512],
                    start=True, stop=True),
                    reads=[r_wT[sub], r_const], writes=[pres])
                P.op("vector", lambda E, ps=ps, sub=sub, half=half:
                     E.tensor_copy(
                         out=y2acc[:, sub * D + half * 512:
                                   sub * D + half * 512 + 512], in_=ps),
                     reads=[pres, r_out], writes=[r_y2[sub][half]])
        P.filler = []
        for e in range(NE):
            for fb in range(4):
                P.filler.append(lambda e=e, fb=fb: _moe_unit(e, fb))

    def _moe_unit(e, fb):
        if True:
            if True:
                slot1, sl1res = wload(m1_s[e, fb], r_m1_s[e])
                for fc in range(NCH):
                    ps, pres = next_ps()
                    fns = []
                    for dc in range(NCH):
                        fns.append(lambda E, ps=ps, slot1=slot1, dc=dc, fc=fc:
                                   E.matmul(
                                       out=ps,
                                       lhsT=slot1[:, dc * D + fc * 128:
                                                  dc * D + fc * 128 + 128],
                                       rhs=xnfm_ch[dc],
                                       start=(dc == 0), stop=(dc == NCH - 1)))
                    P.group("tensor", fns, reads=[sl1res] + r_xnfm,
                            writes=[pres])
                    P.op("scalar", lambda E, ps=ps, e=e, fb=fb, fc=fc:
                         E.activation(out=y1_ch[fc], in_=ps,
                                      func=ACT.Gelu_apprx_tanh,
                                      bias=b1r[:, e * 32 + fb * 8 + fc:
                                               e * 32 + fb * 8 + fc + 1]),
                         reads=[pres, r_const], writes=[r_y1[fc]])
                slot2, sl2res = wload(m2_s[e, fb], r_m2_s[e])
                for sub in range(NSUB):
                    for half in range(2):
                        ps, pres = next_ps()
                        fns = []
                        for fc in range(NCH):
                            fns.append(lambda E, ps=ps, slot2=slot2, fc=fc,
                                       sub=sub, half=half:
                                       E.matmul(
                                           out=ps,
                                           lhsT=y1_ch[fc][:, sub * 128:
                                                          (sub + 1) * 128],
                                           rhs=slot2[:, fc * D + half * 512:
                                                     fc * D + half * 512 + 512],
                                           start=(fc == 0),
                                           stop=(fc == NCH - 1)))
                        P.group("tensor", fns, reads=[sl2res] + r_y1,
                                writes=[pres])
                        im = nxt("moeps", 2)
                        mp = moeps[:, im * 512:(im + 1) * 512]
                        mpres = r_moeps[im]
                        P.op("scalar", lambda E, ps=ps, mp=mp, sub=sub, e=e:
                             E.activation(out=mp, in_=ps, func=ACT.Identity,
                                          scale=wtok[:, sub * 8 + e:
                                                     sub * 8 + e + 1]),
                             reads=[pres, r_wtok[sub]], writes=[mpres])
                        P.op("gpsimd", lambda E, mp=mp, sub=sub, half=half:
                             E.tensor_add(
                                 out=y2acc[:, sub * D + half * 512:
                                           sub * D + half * 512 + 512],
                                 in0=y2acc[:, sub * D + half * 512:
                                           sub * D + half * 512 + 512],
                                 in1=mp),
                             reads=[mpres, r_y2[sub][half]],
                             writes=[r_y2[sub][half]])

    for t in range(NT if "main" not in skip else 0):
        for sub in range(NSUB):
            xv = x_d[t * T + sub * 128:t * T + (sub + 1) * 128, :]
            P.op("gpsimd", lambda E, xv=xv, sub=sub: E.dma_start(
                out=stream[:, sub * D:(sub + 1) * D], in_=xv),
                writes=[r_stream[sub]], dma="xld")
            emit_ln(stream[:, sub * D:(sub + 1) * D],
                    stream[:, sub * D:(sub + 1) * D],
                    sub, r_stream[sub], r_stream[sub])
            cast_transpose_sub(stream, r_stream, xnfm_ch, r_xnfm, a_tm,
                               r_atm, sub)
        if with_moe:
            emit_moe(t)
        else:
            for sub in range(NSUB):
                for half in range(2):
                    P.op("vector", lambda E, sub=sub, half=half: E.memset(
                        y2acc[:, sub * D + half * 512:
                              sub * D + half * 512 + 512], 0.0),
                        reads=[r_out], writes=[r_y2[sub][half]])
        if with_attn:
            for l in range(L):
                src_ch, src_res = (xnfm_ch, r_xnfm) if l == 0 else (sfm_ch,
                                                                    r_sfm)
                if l > 0:
                    cast_transpose(stream, r_stream, sfm_ch, r_sfm, a_tm,
                                   r_atm)
                emit_attn_layer(l, src_ch, src_res)
        P.flush_filler()
        for sub in range(NSUB):
            if "fin" in skip:
                continue
            P.op("vector", lambda E, sub=sub: E.tensor_add(
                out=y2acc[:, sub * D:(sub + 1) * D],
                in0=y2acc[:, sub * D:(sub + 1) * D],
                in1=stream[:, sub * D:(sub + 1) * D]),
                reads=[r_stream[sub], r_y2[sub][0], r_y2[sub][1]],
                writes=[r_y2[sub][0], r_y2[sub][1]])
            for half in range(2):
                xrv = x_d[t * T + sub * 128:t * T + (sub + 1) * 128,
                          half * 512:half * 512 + 512]
                P.op("gpsimd", lambda E, xrv=xrv: E.dma_start(out=xr, in_=xrv),
                     writes=[r_xr], dma="xrld")
                P.op("vector", lambda E, sub=sub, half=half: E.tensor_add(
                    out=y2acc[:, sub * D + half * 512:
                              sub * D + half * 512 + 512],
                    in0=y2acc[:, sub * D + half * 512:
                              sub * D + half * 512 + 512],
                    in1=xr),
                    reads=[r_xr, r_y2[sub][half]],
                    writes=[r_y2[sub][half]])
        for sub in range(NSUB):
            ov = out_d[t * T + sub * 128:t * T + (sub + 1) * 128, :]
            P.op("gpsimd", lambda E, ov=ov, sub=sub: E.dma_start(
                out=ov, in_=y2acc[:, sub * D:(sub + 1) * D]),
                reads=[r_y2[sub][0], r_y2[sub][1]],
                writes=[r_out], dma="outst")

    P.q["gpsimd"].append(
        lambda E, s=P.sem["outst"], v=P.cnt["outst"]: E.wait_ge(s, v))

    P.replay()
    return nc


_CACHE = {}


def _get_program(S_len, L, with_moe, detect_races=False):
    key = (S_len, L, with_moe, detect_races)
    if key not in _CACHE:
        _CACHE[key] = build_program(S_len, L, with_moe, detect_races)
    return _CACHE[key]


def make_inputs(ins, S_len=4096, L=3):
    import ml_dtypes
    f32 = lambda a: np.ascontiguousarray(np.asarray(a, dtype=np.float32))
    base = {
        "Wq": f32(ins["L0_attn_Wq"][:L]),
        "Wk": f32(ins["L0_attn_Wk"][:L]),
        "Wv": f32(ins["L0_attn_Wv"][:L]),
        "Wo": f32(ins["L0_attn_Wo"][:L]),
        "bq": f32(ins["L0_attn_bq"][:L]),
        "bk": f32(ins["L0_attn_bk"][:L]),
        "bv": f32(ins["L0_attn_bv"][:L]),
        "bo": f32(ins["L0_attn_bo"][:L]),
        "omega": f32(ins["L0_attn_omega"][:L]),
        "ln1_g": f32(ins["L0_attn_ln1_g"][:L]),
        "ln1_b": f32(ins["L0_attn_ln1_b"][:L]),
        "ln2_g": f32(ins["L0_attn_ln2_g"][:L]),
        "ln2_b": f32(ins["L0_attn_ln2_b"][:L]),
        "W1": f32(ins["L0_attn_W1"][:L]),
        "bf1": f32(ins["L0_attn_bf1"][:L]),
        "W2": f32(ins["L0_attn_W2"][:L]),
        "bf2": f32(ins["L0_attn_bf2"][:L]),
        "gate_w": f32(ins["L0_ffn_gate_w"]),
        "mw1": f32(ins["L0_ffn_w1"]),
        "mb1": f32(ins["L0_ffn_b1"]),
        "mw2": f32(ins["L0_ffn_w2"]),
        "mb2": f32(ins["L0_ffn_b2"]),
        "ident_bf": np.eye(128, dtype=ml_dtypes.bfloat16),
        "ident_f32": np.eye(128, dtype=np.float32),
        "umask": np.triu(np.ones((128, 128), dtype=np.float32)),
    }
    x = f32(ins["L0_attn_x"])
    B = x.shape[0]
    maps = [dict(base, x=np.ascontiguousarray(x[b, :S_len])) for b in range(B)]
    return maps


def kernel(**inputs):
    S_len, L = 4096, 3
    nc = _get_program(S_len, L, True)
    in_maps = make_inputs(inputs, S_len, L)
    res = run_bass_kernel_spmd(nc, in_maps, core_ids=list(range(8)))
    out = np.stack([np.asarray(r["out"], dtype=np.float32)
                    for r in res.results], axis=0)
    return out


# version 11
# speedup vs baseline: 1.0765x; 1.0465x over previous
import math

import numpy as np

import concourse.bass as bass
import concourse.mybir as mybir
from concourse.bass_utils import run_bass_kernel_spmd

F32 = mybir.dt.float32
BF16 = mybir.dt.bfloat16
AX = mybir.AxisListType
ACT = mybir.ActivationFunctionType
OP = mybir.AluOpType

ENGS = ("sync", "tensor", "vector", "scalar", "gpsimd")

D = 1024
NH = 16
DH = 64
M2 = 64
DFF = 4096
NE = 8
T = 512
C = 128
LN_EPS = 1e-5
ATTN_EPS = 1e-6


class Res:
    __slots__ = ("ws", "rs")

    def __init__(self):
        self.ws = {}
        self.rs = {}

    def rdeps(self):
        return list(self.ws.items())

    def wdeps(self):
        return list(self.ws.items()) + list(self.rs.items())

    def did_read(self, ev):
        e, v = ev
        if self.rs.get(e, 0) < v:
            self.rs[e] = v

    def did_write(self, ev):
        e, v = ev
        if self.ws.get(e, 0) < v:
            self.ws[e] = v


class Prog:
    def __init__(self, nc):
        self.nc = nc
        self.q = {e: [] for e in ENGS}
        self.sem = {e: nc.alloc_semaphore("prog_" + e) for e in ENGS}
        self.cnt = {e: 0 for e in ENGS}
        self.lastwait = {e: {} for e in ENGS}
        self.filler = []
        self.fill_credit = 0.0

    def fill(self, credit):
        if not self.filler:
            return
        self.fill_credit += credit
        while self.fill_credit >= 1.0 and self.filler:
            self.fill_credit -= 1.0
            self.filler.pop(0)()

    def flush_filler(self):
        while self.filler:
            self.filler.pop(0)()

    def stream(self, name):
        self.sem[name] = self.nc.alloc_semaphore("dma_" + name)
        self.cnt[name] = 0

    def _deps(self, eng, reads, writes):
        need = {}
        for r in reads:
            for e2, v in r.rdeps():
                if need.get(e2, 0) < v:
                    need[e2] = v
        for w in writes:
            for e2, v in w.wdeps():
                if need.get(e2, 0) < v:
                    need[e2] = v
        for e2, v in need.items():
            if eng == "tensor" and e2 == "tensor":
                continue
            if self.lastwait[eng].get(e2, 0) < v:
                self.lastwait[eng][e2] = v
                s = self.sem[e2]
                self.q[eng].append(
                    lambda E, s=s, v=v: E.wait_ge(s, v))

    def op(self, eng, fn, reads=(), writes=(), dma=None):
        self._deps(eng, reads, writes)
        key = eng if dma is None else dma
        if dma is not None and self.cnt.get(key, 0) > 0:
            if self.lastwait[eng].get(key, 0) < self.cnt[key]:
                self.lastwait[eng][key] = self.cnt[key]
                s0, v0 = self.sem[key], self.cnt[key]
                self.q[eng].append(
                    lambda E, s0=s0, v0=v0: E.wait_ge(s0, v0))
        k = 1 if dma is None else 16
        self.cnt[key] += k
        s = self.sem[key]
        self.q[eng].append(lambda E, fn=fn, s=s, k=k: fn(E).then_inc(s, k))
        ev = (key, self.cnt[key])
        for r in reads:
            r.did_read(ev)
        for w in writes:
            w.did_write(ev)
        return ev

    def group(self, eng, fns, reads=(), writes=()):
        self._deps(eng, reads, writes)
        self.cnt[eng] += 1
        s = self.sem[eng]
        for fn in fns[:-1]:
            self.q[eng].append(lambda E, fn=fn: fn(E))
        fn = fns[-1]
        self.q[eng].append(lambda E, fn=fn, s=s: fn(E).then_inc(s, 1))
        ev = (eng, self.cnt[eng])
        for r in reads:
            r.did_read(ev)
        for w in writes:
            w.did_write(ev)
        return ev

    def replay(self):
        nc = self.nc
        with nc.Block() as blk:
            @blk.sync
            def _(E):
                for fn in self.q["sync"]:
                    fn(E)

            @blk.tensor
            def _(E):
                for fn in self.q["tensor"]:
                    fn(E)

            @blk.vector
            def _(E):
                for fn in self.q["vector"]:
                    fn(E)

            @blk.scalar
            def _(E):
                for fn in self.q["scalar"]:
                    fn(E)

            @blk.gpsimd
            def _(E):
                for fn in self.q["gpsimd"]:
                    fn(E)


def build_program(S_len=4096, L=3, with_moe=True, detect_races=False,
                  with_attn=True, attn_stop=None, skip=()):
    assert S_len % T == 0
    NT = S_len // T
    NSUB = T // 128
    NCH = 8
    NSLOT = 4

    nc = bass.Bass("TRN2", target_bir_lowering=False,
                   detect_race_conditions=detect_races)

    d_in = {}

    def din(name, shape, dt=F32):
        d_in[name] = nc.dram_tensor(name, list(shape), dt,
                                    kind="ExternalInput").ap()
        return d_in[name]

    x_d = din("x", [S_len, D])
    wq_d = din("Wq", [L, D, D])
    wk_d = din("Wk", [L, D, D])
    wv_d = din("Wv", [L, D, D])
    wo_d = din("Wo", [L, D, D])
    bq_d = din("bq", [L, D])
    bk_d = din("bk", [L, D])
    bv_d = din("bv", [L, D])
    bo_d = din("bo", [L, D])
    om_d = din("omega", [L, DH, 32])
    l1g_d = din("ln1_g", [L, D])
    l1b_d = din("ln1_b", [L, D])
    l2g_d = din("ln2_g", [L, D])
    l2b_d = din("ln2_b", [L, D])
    w1_d = din("W1", [L, D, DFF])
    bf1_d = din("bf1", [L, DFF])
    w2_d = din("W2", [L, DFF, D])
    bf2_d = din("bf2", [L, D])
    gw_d = din("gate_w", [NE, D])
    m1_d = din("mw1", [DFF, NE, D])
    mb1_d = din("mb1", [NE, DFF])
    m2_d = din("mw2", [D, NE, DFF])
    mb2_d = din("mb2", [NE, D])
    idb_d = din("ident_bf", [128, 128], BF16)
    idf_d = din("ident_f32", [128, 128])
    um_d = din("umask", [128, 128])

    out_d = nc.dram_tensor("out", [S_len, D], F32, kind="ExternalOutput").ap()

    wq_s = nc.dram_tensor("wq_s", [L, NCH, 128, D], BF16, kind="Internal").ap()
    wk_s = nc.dram_tensor("wk_s", [L, NCH, 128, D], BF16, kind="Internal").ap()
    wv_s = nc.dram_tensor("wv_s", [L, NCH, 128, D], BF16, kind="Internal").ap()
    wo_s = nc.dram_tensor("wo_s", [L, NCH, 128, D], BF16, kind="Internal").ap()
    w1_s = nc.dram_tensor("w1_s", [L, 4, NCH, 128, D], BF16, kind="Internal").ap()
    w2_s = nc.dram_tensor("w2_s", [L, 4, NCH, 128, D], BF16, kind="Internal").ap()
    m1_s = nc.dram_tensor("m1_s", [NE, 4, NCH, 128, D], BF16, kind="Internal").ap()
    m2_s = nc.dram_tensor("m2_s", [NE, 4, NCH, 128, D], BF16, kind="Internal").ap()
    ln_s = nc.dram_tensor("ln_s", [L, 4, 128, D], F32, kind="Internal").ap()

    def sb(name, shape, dt):
        return nc.alloc_sbuf_tensor(name, list(shape), dt).ap()

    stream = sb("stream", [128, NSUB * D], F32)
    y2acc = sb("y2acc", [128, NSUB * D], F32)
    xn_fm = sb("xn_fm", [128, NCH * T], BF16)
    sfm = sb("sfm", [128, NCH * T], BF16)
    a_fm = sb("a_fm", [128, NCH * T], BF16)
    qf = sb("qf", [128, NCH * T], BF16)
    kf = sb("kf", [128, NCH * T], BF16)
    kft = sb("kft", [128, NSUB * D], BF16)
    a_tm = sb("a_tm", [128, NSUB * D], BF16)
    v_tm = sb("v_tm", [128, NSUB * NH * 65], BF16)
    y1 = sb("y1", [128, NCH * T], BF16)
    q2 = sb("q2", [128, T], BF16)
    s_sc = sb("s_sc", [128, T], F32)
    wslot = sb("wslot", [128, NSLOT * NCH * D], BF16)
    lnrep = sb("lnrep", [128, 2 * D], F32)
    xr = sb("xr", [128, 512], F32)
    sst = sb("sst", [128, L * NH // 2 * 65], F32)
    saug = sb("saug", [128, NH // 2 * 66], BF16)
    gmb = sb("gmb", [128, 2 * 128], BF16)
    ndb = sb("ndb", [128, 2 * 66], F32)
    rcb = sb("rcb", [128, 8], F32)
    stats = sb("stats", [128, NSUB * 24], F32)
    moeps = sb("moeps", [128, 2 * 512], F32)
    wtok = sb("wtok", [128, NSUB * 8], F32)
    prb = sb("prb", [128, 2 * 8], F32)
    wT = sb("wT", [8, 128], F32)
    b2r = sb("b2r", [8, D], F32)
    bqk = sb("bqk", [128, L * 16], F32)
    bf1r = sb("bf1r", [128, L * 32], F32)
    b1r = sb("b1r", [128, NE * 32], F32)
    rowb = [sb("rowb0", [1, D], F32)]
    ones1 = sb("ones1", [1, 128], F32)
    gwT = sb("gwT", [128, NCH * 8], BF16)
    gwT32 = sb("gwT32", [128, NCH * 8], F32)
    omblk = sb("omblk", [128, L * 128], BF16)
    onesblk = sb("onesblk", [128, 128], BF16)
    identb = sb("identb", [128, 128], BF16)
    identf = sb("identf", [128, 128], F32)
    umask = sb("umask_sb", [128, 128], F32)
    omsb = sb("omsb", [128, 32], F32)
    cns = sb("cns", [128, 8], F32)

    psA = nc.alloc_psum_tensor("psA", [128, 512], F32).ap()
    psB = nc.alloc_psum_tensor("psB", [128, 512], F32).ap()
    psT0 = nc.alloc_psum_tensor("psT0", [128, 128], BF16).ap()
    psT1 = nc.alloc_psum_tensor("psT1", [128, 128], BF16).ap()
    psG0 = nc.alloc_psum_tensor("psG0", [128, 128], F32).ap()
    psG1 = nc.alloc_psum_tensor("psG1", [128, 128], F32).ap()
    psN0 = nc.alloc_psum_tensor("psN0", [128, 160], F32).ap()
    psN1b = nc.alloc_psum_tensor("psN1b", [128, 160], F32).ap()
    psT_banks = [psT0, psT1]
    psG_banks = [psG0, psG1]
    psN_banks = [psN0, psN1b]

    P = Prog(nc)
    for _s in ("ws0", "ws1", "ws2", "ws3", "p0a", "p0b", "p0c", "p0o",
               "p0lno", "cload", "lnld", "rowld", "xld", "xrld", "outst"):
        P.stream(_s)

    r_x = Res()
    r_stream = [Res() for _ in range(NSUB)]
    r_y2 = [[Res() for _ in range(2)] for _ in range(NSUB)]
    r_xnfm = [Res() for _ in range(NCH)]
    r_sfm = [Res() for _ in range(NCH)]
    r_afm = [Res() for _ in range(NCH)]
    r_qf = [Res() for _ in range(NCH)]
    r_kf = [Res() for _ in range(NCH)]
    r_kft = [Res() for _ in range(NSUB)]
    r_atm = [Res() for _ in range(NSUB)]
    r_vtm = [[Res() for _ in range(2)] for _ in range(NSUB)]
    r_y1 = [Res() for _ in range(NCH)]
    r_q2 = [Res() for _ in range(2)]
    r_ssc = [Res() for _ in range(2)]
    r_slot = [Res() for _ in range(NSLOT)]
    r_lnrep = Res()
    r_xr = Res()
    r_sst = [Res() for _ in range(NH)]
    r_saug = [Res() for _ in range(NH // 2)]
    r_gmb = [Res() for _ in range(2)]
    r_ndb = [Res() for _ in range(2)]
    r_rcb = [Res() for _ in range(2)]
    r_stats = [Res() for _ in range(NSUB)]
    r_moeps = [Res(), Res()]
    r_wtok = [Res() for _ in range(NSUB)]
    r_prb = [Res() for _ in range(2)]
    _rwT = Res()
    r_wT = [_rwT for _ in range(NSUB)]
    r_psA = Res()
    r_psB = Res()
    r_psG = [Res() for _ in range(2)]
    r_psN1 = [Res() for _ in range(2)]
    r_psN2 = [Res() for _ in range(2)]
    r_psT = [Res() for _ in range(2)]
    r_const = Res()
    r_rowb = [Res()]
    rowb_i = [0]

    def load_row(src_row_ap):
        i = 0
        P.op("gpsimd", lambda E, i=i, a=src_row_ap: E.dma_start(
            out=rowb[i], in_=a), writes=[r_rowb[i]], dma="rowld")
        return rowb[i], r_rowb[i]
    r_out = Res()
    r_wq_s = [Res() for _ in range(L)]
    r_wk_s = [Res() for _ in range(L)]
    r_wv_s = [Res() for _ in range(L)]
    r_wo_s = [Res() for _ in range(L)]
    r_w1_s = [[Res() for _ in range(4)] for _ in range(L)]
    r_w2_s = [[Res() for _ in range(4)] for _ in range(L)]
    r_m1_s = [Res() for _ in range(NE)]
    r_m2_s = [Res() for _ in range(NE)]
    r_ln_s = [[Res() for _ in range(4)] for _ in range(L)]
    r_ph_f32 = [Res() for _ in range(3)]
    r_ph_bf = [Res() for _ in range(4)]

    big_ps = [(psA, r_psA), (psB, r_psB)]
    ps_i = [0]

    def next_ps(n=1):
        out = []
        for _ in range(n):
            out.append(big_ps[ps_i[0] % 2])
            ps_i[0] += 1
        return out if n > 1 else out[0]

    rot = {"G": [0], "N1": [0], "N2": [0], "T": [0], "q2": [0],
           "ssc": [0], "gmb": [0], "ndb": [0], "rcb": [0], "prb": [0],
           "moeps": [0]}

    def nxt(key, n):
        i = rot[key][0] % n
        rot[key][0] += 1
        return i

    ev0 = P.op("vector", lambda E: E.memset(ones1, 1.0), writes=[r_const])
    P.op("vector", lambda E: E.memset(cns[:, 0:1], 64.0), writes=[r_const])
    P.op("scalar", lambda E: E.activation(out=cns[:, 1:2], in_=cns[:, 0:1],
                                          func=ACT.Ln),
         reads=[r_const], writes=[r_const])
    P.op("vector", lambda E: E.tensor_scalar_mul(out=cns[:, 2:3],
                                                 in0=cns[:, 1:2], scalar1=-0.5),
         reads=[r_const], writes=[r_const])
    P.op("vector", lambda E: E.tensor_scalar_mul(out=cns[:, 3:4],
                                                 in0=cns[:, 1:2],
                                                 scalar1=-0.25),
         reads=[r_const], writes=[r_const])
    P.op("scalar", lambda E: E.activation(out=cns[:, 4:5], in_=cns[:, 3:4],
                                          func=ACT.Exp),
         reads=[r_const], writes=[r_const])
    P.op("gpsimd", lambda E: E.dma_start(out=identb, in_=idb_d),
         writes=[r_const], dma="cload")
    P.op("gpsimd", lambda E: E.dma_start(out=identf, in_=idf_d),
         writes=[r_const], dma="cload")
    P.op("gpsimd", lambda E: E.dma_start(out=umask, in_=um_d),
         writes=[r_const], dma="cload")
    for l in range(L):
        for (dd, base) in ((bq_d, 0), (bk_d, 8)):
            P.op("gpsimd", lambda E, dd=dd, l=l, base=base: E.dma_start(
                out=bqk[:, l * 16 + base: l * 16 + base + 8],
                in_=dd[l].rearrange("(c p) -> p c", p=128),
                allow_slow_non_contiguous=True),
                writes=[r_const], dma="cload")
        P.op("gpsimd", lambda E, l=l: E.dma_start(
            out=bf1r[:, l * 32:(l + 1) * 32],
            in_=bf1_d[l].rearrange("(c p) -> p c", p=128),
            allow_slow_non_contiguous=True),
            writes=[r_const], dma="cload")
    if with_moe:
        for e in range(NE):
            P.op("gpsimd", lambda E, e=e: E.dma_start(
                out=b1r[:, e * 32:(e + 1) * 32],
                in_=mb1_d[e].rearrange("(c p) -> p c", p=128),
                allow_slow_non_contiguous=True),
                writes=[r_const], dma="cload")
        P.op("gpsimd", lambda E: E.dma_start(out=b2r, in_=mb2_d),
             writes=[r_const], dma="cload")
        for c in range(NCH):
            P.op("gpsimd", lambda E, c=c: E.dma_start(
                out=gwT32[:, c * 8:(c + 1) * 8],
                in_=gw_d[:, c * 128:(c + 1) * 128].transpose([1, 0]),
                allow_slow_non_contiguous=True),
                writes=[r_const], dma="cload")
        P.op("vector", lambda E: E.tensor_copy(out=gwT, in_=gwT32),
             reads=[r_const], writes=[r_const])
    for l in range(L):
        P.op("gpsimd", lambda E, l=l: E.dma_start(out=omsb[0:64, :],
                                                  in_=om_d[l]),
             writes=[r_const], dma="cload")
        P.op("vector", lambda E: E.tensor_copy(out=omsb[64:128, :],
                                               in_=omsb[0:64, :]),
             reads=[r_const], writes=[r_const])
        stg = stream[:, 0:128]
        P.op("vector", lambda E, stg=stg: E.memset(stg, 0.0),
             writes=[r_ph_f32[1]])
        for (p0, p1, c0, neg) in ((0, 64, 0, False), (0, 64, 32, True),
                                  (64, 128, 64, False), (64, 128, 96, True)):
            sc = cns[:, 5:6] if neg else cns[:, 4:5]
            if neg:
                P.op("vector", lambda E: E.tensor_scalar_mul(
                    out=cns[:, 5:6], in0=cns[:, 4:5], scalar1=-1.0),
                    reads=[r_const], writes=[r_const])
            P.op("vector", lambda E, p0=p0, p1=p1, c0=c0, sc=sc, stg=stg:
                 E.tensor_scalar_mul(out=stg[p0:p1, c0:c0 + 32],
                                     in0=omsb[p0:p1, :], scalar1=sc[p0:p1, :]),
                 reads=[r_const], writes=[r_ph_f32[1]])
        P.op("vector", lambda E, l=l, stg=stg: E.tensor_copy(
            out=omblk[:, l * 128:(l + 1) * 128], in_=stg),
            reads=[r_ph_f32[1]], writes=[r_const])
    P.op("vector", lambda E: E.memset(onesblk, 0.0), writes=[r_const])
    P.op("vector", lambda E: E.memset(onesblk[0:64, 0:64], -0.0625),
         writes=[r_const])
    P.op("vector", lambda E: E.memset(onesblk[64:128, 64:128], -0.0625),
         writes=[r_const])
    for sub in range(NSUB):
        base = sub * NH * 65
        ap = v_tm[:, base:base + NH * 65].rearrange("p (h c) -> p h c", h=NH)
        P.op("vector", lambda E, ap=ap: E.memset(ap[:, :, 64:65], 1.0),
             writes=[r_vtm[sub][0], r_vtm[sub][1]])
    P.op("vector", lambda E: E.memset(sst, 0.0), writes=r_sst)

    do_cast = "cast" not in skip
    ph_f32 = [(stream, r_ph_f32[1], "p0b", "gpsimd"),
              (y2acc, r_ph_f32[2], "p0c", "sync")]
    ph_bf = [(sfm, r_ph_bf[0]), (a_fm, r_ph_bf[1]), (qf, r_ph_bf[2]),
             (kf, r_ph_bf[3])]
    ph_mc = [(y1, r_y1[0]), (kft, r_kft[0]), (a_tm, r_atm[0])]
    ph_i = [0]
    bf_i = [0]
    mc_i = [0]

    def f32buf():
        i = ph_i[0] % 2
        ph_i[0] += 1
        return ph_f32[i]

    def bfbuf():
        i = bf_i[0] % 4
        bf_i[0] += 1
        return ph_bf[i]

    def mcbuf():
        i = mc_i[0] % 3
        mc_i[0] += 1
        return ph_mc[i]

    def cast_block(in_view, out_view, ncols, scr_res):
        fb, fres, fstr, fq = f32buf()
        bb, bres = bfbuf()
        n = ncols * 1024
        fv = fb[:, 0:n].rearrange("p (c f) -> p c f", c=ncols)
        bv = bb[:, 0:n].rearrange("p (c f) -> p c f", c=ncols)
        P.op(fq, lambda E, fv=fv, in_view=in_view: E.dma_start(
            out=fv, in_=in_view), writes=[fres], dma=fstr)
        P.op("vector", lambda E, bb=bb, fb=fb, n=n: E.tensor_copy(
            out=bb[:, 0:n], in_=fb[:, 0:n]), reads=[fres], writes=[bres])
        P.op("scalar", lambda E, out_view=out_view, bv=bv: E.dma_start(
            out=out_view, in_=bv), reads=[bres], writes=[scr_res],
            dma="p0o")

    def conv_attn_layer(l):
        for (src_, scr, rres) in ((wq_d, wq_s, r_wq_s), (wk_d, wk_s, r_wk_s),
                                  (wv_d, wv_s, r_wv_s), (wo_d, wo_s, r_wo_s)):
            for dc in range(0, NCH, 4):
                cast_block(
                    src_[l, dc * 128:(dc + 4) * 128, :]
                    .rearrange("(c p) f -> p c f", c=4),
                    scr[l, dc:dc + 4].transpose([1, 0, 2]),
                    4, rres[l])
        for dc in range(NCH):
            cast_block(
                w1_d[l, dc * 128:(dc + 1) * 128, :]
                .rearrange("p (c f) -> p c f", c=4),
                w1_s[l, :, dc].transpose([1, 0, 2]),
                4, r_w1_s[l][0])
        for j in range(NCH):
            cast_block(
                w2_d[l, j * 512:(j + 1) * 512, :]
                .rearrange("(c p) f -> p c f", c=4),
                w2_s[l].rearrange("a b p f -> (a b) p f")[j * 4:(j + 1) * 4]
                .transpose([1, 0, 2]),
                4, r_w2_s[l][0])

    tp_i = [0]

    def tp_slot():
        i = tp_i[0] % 2
        tp_i[0] += 1
        return psT_banks[i], r_psT[i]

    def conv_moe_expert(e):
        for fc in range(DFF // 128):
            fb_, fres, fstr, fq = f32buf()
            bb_, bres = bfbuf()
            mb_, mres = mcbuf()
            P.op(fq, lambda E, fc=fc, e=e, fb_=fb_: E.dma_start(
                out=fb_[:, 0:D], in_=m1_d[fc * 128:(fc + 1) * 128, e, :]),
                writes=[fres], dma=fstr)
            P.op("vector", lambda E, fb_=fb_, mb_=mb_: E.tensor_copy(
                out=mb_[:, 0:D], in_=fb_[:, 0:D]),
                reads=[fres], writes=[mres])
            for dc in range(NCH):
                ps, pres = tp_slot()
                P.op("tensor", lambda E, ps=ps, dc=dc, mb_=mb_:
                     E.transpose(out=ps,
                                 in_=mb_[:, dc * 128:(dc + 1) * 128],
                                 identity=identb),
                     reads=[mres, r_const], writes=[pres])
                P.op("vector", lambda E, bb_=bb_, dc=dc, ps=ps:
                     E.tensor_copy(out=bb_[:, dc * 128:(dc + 1) * 128],
                                   in_=ps),
                     reads=[pres], writes=[bres])
            ov = m1_s[e, fc // 8, :, :, (fc % 8) * 128:(fc % 8 + 1) * 128]\
                .transpose([1, 0, 2])
            iv = bb_[:, 0:1024].rearrange("p (c f) -> p c f", c=8)
            P.op("scalar", lambda E, ov=ov, iv=iv: E.dma_start(
                out=ov, in_=iv), reads=[bres], writes=[r_m1_s[e]],
                dma="p0o")
        for dc in range(NCH):
            fb_, fres, fstr, fq = f32buf()
            bb_, bres = bfbuf()
            mb_, mres = mcbuf()
            P.op(fq, lambda E, dc=dc, e=e, fb_=fb_: E.dma_start(
                out=fb_[:, 0:DFF],
                in_=m2_d[dc * 128:(dc + 1) * 128, e, :]),
                writes=[fres], dma=fstr)
            P.op("vector", lambda E, fb_=fb_, mb_=mb_: E.tensor_copy(
                out=mb_[:, 0:DFF], in_=fb_[:, 0:DFF]),
                reads=[fres], writes=[mres])
            for fc in range(DFF // 128):
                ps, pres = tp_slot()
                P.op("tensor", lambda E, ps=ps, fc=fc, mb_=mb_:
                     E.transpose(out=ps,
                                 in_=mb_[:, fc * 128:(fc + 1) * 128],
                                 identity=identb),
                     reads=[mres, r_const], writes=[pres])
                P.op("vector", lambda E, bb_=bb_, fc=fc, ps=ps:
                     E.tensor_copy(out=bb_[:, fc * 128:(fc + 1) * 128],
                                   in_=ps),
                     reads=[pres], writes=[bres])
            ov = m2_s[e, :, :, :, dc * 128:(dc + 1) * 128]\
                .transpose([2, 0, 1, 3])
            iv = bb_[:, 0:DFF].rearrange("p (a b f) -> p a b f", a=4, b=8)
            P.op("scalar", lambda E, ov=ov, iv=iv: E.dma_start(
                out=ov, in_=iv), reads=[bres], writes=[r_m2_s[e]],
                dma="p0o")

    lnrow = rowb[0]
    r_lnrow = r_rowb[0]
    for l in range(L if "lnrep" not in skip else 0):
        for (j, dd) in ((0, l1g_d), (1, l1b_d), (2, l2g_d), (3, l2b_d)):
            P.op("gpsimd", lambda E, dd=dd, l=l: E.dma_start(
                out=lnrow, in_=dd[l:l + 1, :]),
                writes=[r_lnrow], dma="rowld")
            fb_, fres, fstr, fq = f32buf()
            for h in range(2):
                P.op("tensor", lambda E, h=h: E.matmul(
                    out=(psA if h == 0 else psB)[:, 0:512], lhsT=ones1,
                    rhs=lnrow[:, h * 512:(h + 1) * 512],
                    start=True, stop=True),
                    reads=[r_lnrow, r_const],
                    writes=[r_psA if h == 0 else r_psB])
                P.op("vector", lambda E, h=h, fb_=fb_: E.tensor_copy(
                    out=fb_[:, h * 512:(h + 1) * 512],
                    in_=(psA if h == 0 else psB)[:, 0:512]),
                    reads=[r_psA if h == 0 else r_psB], writes=[fres])
            P.op("gpsimd", lambda E, l=l, j=j, fb_=fb_: E.dma_start(
                out=ln_s[l, j], in_=fb_[:, 0:D]),
                reads=[fres], writes=[r_ln_s[l][j]], dma="p0lno")

    conv_order = []
    for l in range(L if do_cast else 0):
        conv_order.append(("attn", l))
        if with_moe:
            if l == 0:
                conv_order += [("moe", 0), ("moe", 1)]
            elif l == 1:
                conv_order += [("moe", 2), ("moe", 3)]
    if with_moe:
        done = {e for k, e in conv_order if k == "moe"}
        conv_order += [("moe", e) for e in range(NE) if e not in done]
    for kind, idx in conv_order:
        if kind == "attn":
            conv_attn_layer(idx)
        else:
            conv_moe_expert(idx)
    for l in range(L):
        for fb in range(4):
            r_w1_s[l][fb] = r_w1_s[l][0]
            r_w2_s[l][fb] = r_w2_s[l][0]

    def _merge(phase, mains):
        for m in mains:
            for k, v in phase.ws.items():
                if m.rs.get(k, 0) < v:
                    m.rs[k] = v
                if m.ws.get(k, 0) < v:
                    m.ws[k] = v
            for k, v in phase.rs.items():
                if m.rs.get(k, 0) < v:
                    m.rs[k] = v
    _merge(r_ph_f32[1], r_stream)
    _merge(r_ph_f32[2], [r_y2[s][h] for s in range(NSUB) for h in range(2)])
    _merge(r_ph_bf[0], r_sfm)
    _merge(r_ph_bf[1], r_afm)
    _merge(r_ph_bf[2], r_qf)
    _merge(r_ph_bf[3], r_kf)
    _merge(r_y1[0], r_y1[1:])
    _merge(r_kft[0], r_kft[1:])
    _merge(r_atm[0], r_atm[1:])

    slot_i = [0]

    def wload(scr_view, scr_res):
        i = slot_i[0] % NSLOT
        slot_i[0] += 1
        sl = wslot[:, i * NCH * D:(i + 1) * NCH * D]
        res = r_slot[i]
        ov = sl.rearrange("p (c f) -> p c f", c=NCH)
        iv = scr_view.transpose([1, 0, 2])
        P.op("sync", lambda E, ov=ov, iv=iv: E.dma_start(
            out=ov[:, 0:NCH // 2, :], in_=iv[:, 0:NCH // 2, :]),
            reads=[scr_res], writes=[res], dma="ws%d" % i)
        P.op("sync", lambda E, ov=ov, iv=iv: E.dma_start(
            out=ov[:, NCH // 2:, :], in_=iv[:, NCH // 2:, :]),
            reads=[scr_res], writes=[res], dma="ws%d" % i)
        return sl, res

    def emit_ln(ap_in, ap_out, sub, res_in, res_out, gamma=None, beta=None,
               res_gb=None):
        st = stats[:, sub * 24:(sub + 1) * 24]
        sres = r_stats[sub]
        P.op("vector", lambda E: E.bn_stats(out=st[:, 0:6], in_=ap_in[:, 0:512]),
             reads=[res_in], writes=[sres])
        P.op("vector", lambda E: E.bn_stats(out=st[:, 6:12],
                                            in_=ap_in[:, 512:1024]),
             reads=[res_in], writes=[sres])
        P.op("vector", lambda E: E.bn_aggr(out=st[:, 12:14], in_=st[:, 0:12]),
             reads=[sres], writes=[sres])
        P.op("vector", lambda E: E.tensor_scalar_add(out=st[:, 14:15],
                                                     in0=st[:, 13:14],
                                                     scalar1=LN_EPS),
             reads=[sres], writes=[sres])
        P.op("vector", lambda E: E.reciprocal(out=st[:, 15:16],
                                              in_=st[:, 14:15]),
             reads=[sres], writes=[sres])
        P.op("scalar", lambda E: E.activation(out=st[:, 16:17],
                                              in_=st[:, 15:16], func=ACT.Sqrt),
             reads=[sres], writes=[sres])
        P.op("vector", lambda E: E.tensor_scalar(
            out=ap_out, in0=ap_in, scalar1=st[:, 12:13],
            scalar2=st[:, 16:17], op0=OP.subtract, op1=OP.mult),
            reads=[res_in, sres], writes=[res_out])
        if gamma is not None:
            P.op("gpsimd", lambda E: E.tensor_mul(out=ap_out, in0=ap_out,
                                                  in1=gamma),
                 reads=[res_gb, res_out], writes=[res_out])
            P.op("gpsimd", lambda E: E.tensor_add(out=ap_out, in0=ap_out,
                                                  in1=beta),
                 reads=[res_gb, res_out], writes=[res_out])
        P.fill(0.16)

    def cast_transpose(src_ap, src_res_list, dst_ap_chunks, dst_res,
                       scratch_ap, scratch_res):
        for sub in range(NSUB):
            cast_transpose_sub(src_ap, src_res_list, dst_ap_chunks, dst_res,
                               scratch_ap, scratch_res, sub)

    def cast_transpose_sub(src_ap, src_res_list, dst_ap_chunks, dst_res,
                           scratch_ap, scratch_res, sub):
        if True:
            P.op("vector", lambda E, sub=sub: E.tensor_copy(
                out=scratch_ap[:, sub * D:(sub + 1) * D],
                in_=src_ap[:, sub * D:(sub + 1) * D]),
                reads=[src_res_list[sub]], writes=[scratch_res[sub]])
            if "ctp_cast" in skip:
                return
            for p in range(NCH):
                i = nxt("T", 2)
                ps = psT_banks[i]
                pres = r_psT[i]
                P.op("tensor", lambda E, ps=ps, sub=sub, p=p: E.transpose(
                    out=ps,
                    in_=scratch_ap[:, sub * D + p * 128: sub * D + (p + 1) * 128],
                    identity=identb),
                    reads=[scratch_res[sub], r_const], writes=[pres])
                if "ctp_evac" in skip:
                    continue
                P.op("vector", lambda E, ps=ps, sub=sub, p=p: E.tensor_copy(
                    out=dst_ap_chunks[p][:, sub * 128:(sub + 1) * 128],
                    in_=ps),
                    reads=[pres], writes=[dst_res[p]])
                P.fill(0.033)

    def chunks_of(buf, n, w):
        return [buf[:, i * w:(i + 1) * w] for i in range(n)]

    xnfm_ch = chunks_of(xn_fm, NCH, T)
    sfm_ch = chunks_of(sfm, NCH, T)
    afm_ch = chunks_of(a_fm, NCH, T)
    qf_ch = chunks_of(qf, NCH, T)
    kf_ch = chunks_of(kf, NCH, T)
    y1_ch = chunks_of(y1, NCH, T)

    def proj_fm(dst_ch, dst_res, slot, slot_res, src_ch, src_res, bias_col):
        for pout in range(NCH):
            ps, pres = next_ps()
            fns = []
            for dc in range(NCH):
                fns.append(lambda E, ps=ps, slot=slot, dc=dc, pout=pout:
                           E.matmul(out=ps, lhsT=slot[:, dc * D + pout * 128:
                                                      dc * D + pout * 128 + 128],
                                    rhs=src_ch[dc],
                                    start=(dc == 0), stop=(dc == NCH - 1)))
            P.group("tensor", fns,
                    reads=[slot_res, r_const] + src_res, writes=[pres])
            P.op("scalar", lambda E, ps=ps, pout=pout: E.activation(
                out=dst_ch[pout], in_=ps, func=ACT.Identity,
                bias=bias_col[:, pout:pout + 1]),
                reads=[pres, r_const], writes=[dst_res[pout]])

    def favor(src_ch, src_res, om_ap):
        for p in range(NCH):
            q2a = q2
            q2res = r_q2[0]
            P.op("vector", lambda E, p=p, q2a=q2a: E.tensor_mul(
                out=q2a, in0=src_ch[p], in1=src_ch[p]),
                reads=[src_res[p]], writes=[q2res])
            psu, ures = next_ps()
            P.group("tensor", [
                lambda E, psu=psu, p=p, om_ap=om_ap: E.matmul(
                    out=psu, lhsT=om_ap, rhs=src_ch[p],
                    start=True, stop=False),
                lambda E, psu=psu, q2a=q2a: E.matmul(
                    out=psu, lhsT=onesblk, rhs=q2a, start=False, stop=True)],
                reads=[src_res[p], q2res, r_const], writes=[ures])
            P.op("scalar", lambda E, p=p, psu=psu: E.activation(
                out=src_ch[p], in_=psu, func=ACT.Exp),
                reads=[ures, r_const], writes=[src_res[p]])
            P.fill(0.10)

    def v_aug_ap(sub, h):
        base = sub * NH * 65 + h * 65
        return v_tm[:, base:base + 65]

    def emit_attn_layer(l, src_ch, src_res):
        om_ap = omblk[:, l * 128:(l + 1) * 128]
        P.op("gpsimd", lambda E, l=l: E.dma_start(
            out=lnrep.rearrange("p (j f) -> p j f", j=2), in_=ln_s[l, 0:2]
            .transpose([1, 0, 2])),
            reads=[r_ln_s[l][0], r_ln_s[l][1]], writes=[r_lnrep],
            dma="lnld")
        slot, slres = wload(wq_s[l], r_wq_s[l])
        proj_fm(qf_ch, r_qf, slot, slres, src_ch, src_res,
                bqk[:, l * 16:l * 16 + 8])
        slot, slres = wload(wk_s[l], r_wk_s[l])
        proj_fm(kf_ch, r_kf, slot, slres, src_ch, src_res,
                bqk[:, l * 16 + 8:l * 16 + 16])
        slot, slres = wload(wv_s[l], r_wv_s[l])
        bvr, bvres = load_row(bv_d[l:l + 1, :])
        for sub in range(NSUB):
            for half in range(2):
                ps, pres = next_ps()
                fns = []
                for dc in range(NCH):
                    fns.append(lambda E, ps=ps, slot=slot, dc=dc, sub=sub,
                               half=half:
                               E.matmul(out=ps,
                                        lhsT=src_ch[dc][:, sub * 128:
                                                        (sub + 1) * 128],
                                        rhs=slot[:, dc * D + half * 512:
                                                 dc * D + half * 512 + 512],
                                        start=(dc == 0), stop=False))
                fns.append(lambda E, ps=ps, half=half, bvr=bvr:
                           E.matmul(out=ps, lhsT=ones1,
                                    rhs=bvr[:, half * 512:half * 512 + 512],
                                    start=False, stop=True))
                P.group("tensor", fns,
                        reads=[slres, r_const, bvres] + src_res, writes=[pres])
                ov = v_tm[:, sub * NH * 65 + half * 8 * 65:
                          sub * NH * 65 + (half * 8 + 8) * 65]\
                    .rearrange("p (h c) -> p h c", h=8)[:, :, 0:64]
                P.op("vector", lambda E, ov=ov, ps=ps: E.tensor_copy(
                    out=ov, in_=ps.rearrange("p (h c) -> p h c", h=8)),
                    reads=[pres], writes=[r_vtm[sub][half]])
        if attn_stop == "qkv":
            return
        favor(qf_ch, r_qf, om_ap)
        favor(kf_ch, r_kf, om_ap)
        if attn_stop == "favor":
            return
        for c in range(NSUB):
            for p in range(NCH):
                if "kt_skip" in skip:
                    continue
                i = nxt("T", 2)
                ps = psT_banks[i]
                pres = r_psT[i]
                if "kt_oneslot" in skip:
                    i = 0
                    ps = psT_banks[0]
                    pres = r_psT[0]
                src_tp = identb if "kt_const" in skip \
                    else kf_ch[p][:, c * 128:(c + 1) * 128]
                P.op("tensor", lambda E, ps=ps, src_tp=src_tp: E.transpose(
                    out=ps, in_=src_tp,
                    identity=identb),
                    reads=[r_kf[p], r_const], writes=[pres])
                if "kt_evac" in skip:
                    continue
                P.op("vector", lambda E, ps=ps, c=c, p=p: E.tensor_copy(
                    out=kft[:, c * D + p * 128:c * D + (p + 1) * 128], in_=ps),
                    reads=[pres], writes=[r_kft[c]])
                P.fill(0.033)
        if attn_stop == "kft":
            return
        for c in range(NSUB):
            for p in range(NH // 2):
                P.op("vector", lambda E, p=p, l=l: E.tensor_copy(
                    out=saug[:, p * 66:p * 66 + 65],
                    in_=sst[:, (l * 8 + p) * 65:(l * 8 + p + 1) * 65]),
                    reads=[r_sst[2 * p], r_sst[2 * p + 1]], writes=[r_saug[p]])
            def emit_g(h, c=c):
                p, par = h // 2, h % 2
                r0 = 64 * par
                ig = nxt("G", 2)
                gps = psG_banks[ig]
                gres = r_psG[ig]
                P.op("tensor", lambda E, gps=gps, p=p, r0=r0, c=c: E.matmul(
                    out=gps,
                    lhsT=kf_ch[p][r0:r0 + 64, c * 128:(c + 1) * 128],
                    rhs=qf_ch[p][r0:r0 + 64, c * 128:(c + 1) * 128],
                    start=True, stop=True),
                    reads=[r_kf[p], r_qf[p]], writes=[gres])
                return gps, gres

            for h in range(NH):
                p, par = h // 2, h % 2
                r0 = 64 * par
                gps, gres = emit_g(h)
                im = nxt("gmb", 2)
                gm = gmb[:, im * 128:(im + 1) * 128]
                gmres = r_gmb[im]
                P.op("vector", lambda E, gm=gm, gps=gps: E.tensor_mul(
                    out=gm, in0=gps, in1=umask),
                    reads=[gres, r_const], writes=[gmres])
                i1 = h % 2
                n1 = psN_banks[i1][:, 0:65]
                n1res = r_psN1[i1]
                P.op("tensor", lambda E, n1=n1, gm=gm, c=c, h=h: E.matmul(
                    out=n1, lhsT=gm, rhs=v_aug_ap(c, h),
                    start=True, stop=True),
                    reads=[gmres, r_vtm[c][h // 8]], writes=[n1res])
                n2 = psN_banks[i1][:, 80:145]
                n2res = r_psN2[i1]
                P.op("tensor", lambda E, n2=n2, p=p, r0=r0, c=c: E.matmul(
                    out=n2,
                    lhsT=qf_ch[p][r0:r0 + 64, c * 128:(c + 1) * 128],
                    rhs=saug[r0:r0 + 64, p * 66:p * 66 + 65],
                    start=True, stop=True),
                    reads=[r_qf[p], r_saug[p]], writes=[n2res])
                ind = nxt("ndb", 2)
                nd = ndb[:, ind * 66:ind * 66 + 65]
                ndres = r_ndb[ind]
                P.op("scalar", lambda E, nd=nd, n2=n2: E.copy(
                    out=nd, in_=n2),
                    reads=[n2res], writes=[ndres])
                P.op("vector", lambda E, nd=nd, n1=n1: E.tensor_add(
                    out=nd, in0=nd, in1=n1),
                    reads=[n1res, ndres], writes=[ndres])
                irc = nxt("rcb", 2)
                rc = rcb[:, irc * 2:irc * 2 + 2]
                rcres = r_rcb[irc]
                P.op("vector", lambda E, rc=rc, nd=nd: E.tensor_scalar_add(
                    out=rc[:, 0:1], in0=nd[:, 64:65], scalar1=64 * ATTN_EPS),
                    reads=[ndres], writes=[rcres])
                P.op("vector", lambda E, rc=rc: E.reciprocal(
                    out=rc[:, 1:2], in_=rc[:, 0:1]),
                    reads=[rcres], writes=[rcres])
                P.op("scalar", lambda E, rc=rc, nd=nd, c=c, h=h:
                     E.activation(
                         out=a_tm[:, c * D + h * 64:c * D + (h + 1) * 64],
                         in_=nd[:, 0:64], func=ACT.Identity,
                         scale=rc[:, 1:2]),
                     reads=[ndres, rcres], writes=[r_atm[c]])
                dsfull, dsres = next_ps()
                ds = dsfull[r0:r0 + 64, 0:65]
                P.op("tensor", lambda E, ds=ds, c=c, h=h: E.matmul(
                    out=ds,
                    lhsT=kft[:, c * D + h * 64:c * D + (h + 1) * 64],
                    rhs=v_aug_ap(c, h), start=True, stop=True),
                    reads=[r_kft[c], r_vtm[c][h // 8]], writes=[dsres])
                P.op("vector", lambda E, ds=ds, r0=r0, p=p, l=l, h=h:
                     E.tensor_add(
                         out=sst[r0:r0 + 64,
                                 (l * 8 + p) * 65:(l * 8 + p + 1) * 65],
                         in0=sst[r0:r0 + 64,
                                 (l * 8 + p) * 65:(l * 8 + p + 1) * 65],
                         in1=ds),
                     reads=[dsres, r_sst[h]], writes=[r_sst[h]])
                P.fill(0.10)
        if attn_stop == "core":
            return
        for c in range(NSUB):
            for p in range(NCH):
                i = nxt("T", 2)
                ps = psT_banks[i]
                pres = r_psT[i]
                P.op("tensor", lambda E, ps=ps, c=c, p=p: E.transpose(
                    out=ps, in_=a_tm[:, c * D + p * 128:c * D + (p + 1) * 128],
                    identity=identb),
                    reads=[r_atm[c], r_const], writes=[pres])
                P.op("vector", lambda E, ps=ps, c=c, p=p: E.tensor_copy(
                    out=afm_ch[p][:, c * 128:(c + 1) * 128], in_=ps),
                    reads=[pres], writes=[r_afm[p]])
                P.fill(0.033)
        if attn_stop == "atp":
            return
        slot, slres = wload(wo_s[l], r_wo_s[l])
        bor, bores = load_row(bo_d[l:l + 1, :])
        for sub in range(NSUB):
            for half in range(2):
                ps, pres = next_ps()
                fns = []
                for dc in range(NCH):
                    fns.append(lambda E, ps=ps, slot=slot, dc=dc, sub=sub,
                               half=half:
                               E.matmul(out=ps,
                                        lhsT=afm_ch[dc][:, sub * 128:
                                                        (sub + 1) * 128],
                                        rhs=slot[:, dc * D + half * 512:
                                                 dc * D + half * 512 + 512],
                                        start=(dc == 0), stop=False))
                fns.append(lambda E, ps=ps, half=half, bor=bor:
                           E.matmul(out=ps, lhsT=ones1,
                                    rhs=bor[:, half * 512:half * 512 + 512],
                                    start=False, stop=True))
                P.group("tensor", fns,
                        reads=[slres, r_const, bores] + r_afm, writes=[pres])
                P.op("vector", lambda E, ps=ps, sub=sub, half=half:
                     E.tensor_add(
                         out=stream[:, sub * D + half * 512:
                                    sub * D + half * 512 + 512],
                         in0=stream[:, sub * D + half * 512:
                                    sub * D + half * 512 + 512],
                         in1=ps),
                     reads=[pres, r_stream[sub]], writes=[r_stream[sub]])
        for sub in range(NSUB):
            ap = stream[:, sub * D:(sub + 1) * D]
            emit_ln(ap, ap, sub, r_stream[sub], r_stream[sub],
                    gamma=lnrep[:, 0:D], beta=lnrep[:, D:2 * D],
                    res_gb=r_lnrep)
        P.op("gpsimd", lambda E, l=l: E.dma_start(
            out=lnrep.rearrange("p (j f) -> p j f", j=2), in_=ln_s[l, 2:4]
            .transpose([1, 0, 2])),
            reads=[r_ln_s[l][2], r_ln_s[l][3]], writes=[r_lnrep],
            dma="lnld")
        if attn_stop == "ln1":
            return
        cast_transpose(stream, r_stream, sfm_ch, r_sfm, a_tm, r_atm)
        bfr, bfres = load_row(bf2_d[l:l + 1, :])
        for fb in range(4):
            slot1, sl1res = wload(w1_s[l, fb], r_w1_s[l][fb])
            for fc in range(NCH):
                ps, pres = next_ps()
                fns = []
                for dc in range(NCH):
                    fns.append(lambda E, ps=ps, slot1=slot1, dc=dc, fc=fc:
                               E.matmul(out=ps,
                                        lhsT=slot1[:, dc * D + fc * 128:
                                                   dc * D + fc * 128 + 128],
                                        rhs=sfm_ch[dc],
                                        start=(dc == 0), stop=(dc == NCH - 1)))
                P.group("tensor", fns, reads=[sl1res] + r_sfm, writes=[pres])
                P.op("scalar", lambda E, ps=ps, fc=fc, fb=fb, l=l:
                     E.activation(out=y1_ch[fc], in_=ps, func=ACT.Relu,
                                  bias=bf1r[:, l * 32 + fb * 8 + fc:
                                            l * 32 + fb * 8 + fc + 1]),
                     reads=[pres, r_const], writes=[r_y1[fc]])
            slot2, sl2res = wload(w2_s[l, fb], r_w2_s[l][fb])
            for sub in range(NSUB):
                for half in range(2):
                    ps, pres = next_ps()
                    fns = []
                    for fc in range(NCH):
                        fns.append(lambda E, ps=ps, slot2=slot2, fc=fc,
                                   sub=sub, half=half:
                                   E.matmul(out=ps,
                                            lhsT=y1_ch[fc][:, sub * 128:
                                                           (sub + 1) * 128],
                                            rhs=slot2[:, fc * D + half * 512:
                                                      fc * D + half * 512 + 512],
                                            start=(fc == 0), stop=False))
                    if fb == 0:
                        fns.append(lambda E, ps=ps, half=half, bfr=bfr:
                                   E.matmul(out=ps, lhsT=ones1,
                                            rhs=bfr[:, half * 512:
                                                    half * 512 + 512],
                                            start=False, stop=True))
                    else:
                        prev = fns.pop()
                        fns.append(lambda E, ps=ps, slot2=slot2, sub=sub,
                                   half=half:
                                   E.matmul(out=ps,
                                            lhsT=y1_ch[NCH - 1][:, sub * 128:
                                                                (sub + 1) * 128],
                                            rhs=slot2[:, (NCH - 1) * D +
                                                      half * 512:
                                                      (NCH - 1) * D +
                                                      half * 512 + 512],
                                            start=False, stop=True))
                    P.group("tensor", fns,
                            reads=[sl2res, r_const, bfres] + r_y1,
                            writes=[pres])
                    P.op("vector", lambda E, ps=ps, sub=sub, half=half:
                         E.tensor_add(
                             out=stream[:, sub * D + half * 512:
                                        sub * D + half * 512 + 512],
                             in0=stream[:, sub * D + half * 512:
                                        sub * D + half * 512 + 512],
                             in1=ps),
                         reads=[pres, r_stream[sub]], writes=[r_stream[sub]])
        if attn_stop == "ffn":
            return
        for sub in range(NSUB):
            ap = stream[:, sub * D:(sub + 1) * D]
            emit_ln(ap, ap, sub, r_stream[sub], r_stream[sub],
                    gamma=lnrep[:, 0:D], beta=lnrep[:, D:2 * D],
                    res_gb=r_lnrep)

    def emit_moe(t):
        for sub in range(NSUB):
            ig = nxt("G", 2)
            gps = psG_banks[ig][:, 0:8]
            gres = r_psG[ig]
            fns = []
            for p in range(NCH):
                fns.append(lambda E, gps=gps, p=p, sub=sub:
                           E.matmul(out=gps,
                                    lhsT=xnfm_ch[p][:, sub * 128:
                                                    (sub + 1) * 128],
                                    rhs=gwT[:, p * 8:(p + 1) * 8],
                                    start=(p == 0), stop=(p == NCH - 1)))
            P.group("tensor", fns, reads=[r_const] + r_xnfm, writes=[gres])
            st = stats[:, sub * 24:(sub + 1) * 24]
            sres = r_stats[sub]
            ip = nxt("prb", 2)
            pr = prb[:, ip * 8:(ip + 1) * 8]
            prres = r_prb[ip]
            P.op("vector", lambda E, gps=gps, pr=pr: E.tensor_copy(
                out=pr, in_=gps), reads=[gres], writes=[prres])
            P.op("vector", lambda E, pr=pr, st=st: E.reduce_max(
                out=st[:, 17:18], in_=pr, axis=AX.X),
                reads=[prres], writes=[sres])
            P.op("vector", lambda E, st=st: E.tensor_scalar_mul(
                out=st[:, 18:19], in0=st[:, 17:18], scalar1=-1.0),
                reads=[sres], writes=[sres])
            P.op("scalar", lambda E, pr=pr, st=st: E.activation(
                out=pr, in_=pr, func=ACT.Exp, bias=st[:, 18:19]),
                reads=[sres, prres], writes=[prres])
            P.op("vector", lambda E, pr=pr, st=st: E.reduce_sum(
                out=st[:, 19:20], in_=pr, axis=AX.X),
                reads=[prres], writes=[sres])
            P.op("vector", lambda E, st=st: E.reciprocal(
                out=st[:, 20:21], in_=st[:, 19:20]),
                reads=[sres], writes=[sres])
            P.op("vector", lambda E, pr=pr, st=st: E.tensor_scalar_mul(
                out=pr, in0=pr, scalar1=st[:, 20:21]),
                reads=[sres, prres], writes=[prres])
            P.op("vector", lambda E, pr=pr, st=st: E.reduce_max(
                out=st[:, 21:22], in_=pr, axis=AX.X),
                reads=[prres], writes=[sres])
            ip2 = nxt("prb", 2)
            t8 = prb[:, ip2 * 8:(ip2 + 1) * 8]
            t8res = r_prb[ip2]
            P.op("vector", lambda E, pr=pr, st=st, t8=t8: E.tensor_scalar(
                out=t8, in0=pr, scalar1=st[:, 21:22], scalar2=1e30,
                op0=OP.is_equal, op1=OP.mult),
                reads=[prres, sres], writes=[t8res])
            P.op("vector", lambda E, pr=pr, t8=t8: E.tensor_sub(
                out=t8, in0=pr, in1=t8),
                reads=[prres, t8res], writes=[t8res])
            P.op("vector", lambda E, t8=t8, st=st: E.reduce_max(
                out=st[:, 22:23], in_=t8, axis=AX.X),
                reads=[t8res], writes=[sres])
            P.op("vector", lambda E, pr=pr, st=st, t8=t8: E.tensor_scalar(
                out=t8, in0=pr, scalar1=st[:, 22:23], scalar2=None,
                op0=OP.is_ge),
                reads=[prres, sres], writes=[t8res])
            P.op("vector", lambda E, pr=pr, t8=t8, sub=sub: E.tensor_mul(
                out=wtok[:, sub * 8:(sub + 1) * 8], in0=pr, in1=t8),
                reads=[prres, t8res], writes=[r_wtok[sub]])
            ig2 = nxt("G", 2)
            tps = psG_banks[ig2][0:8, 0:128]
            tres = r_psG[ig2]
            P.op("tensor", lambda E, tps=tps, sub=sub: E.transpose(
                out=tps, in_=wtok[:, sub * 8:(sub + 1) * 8],
                identity=identf),
                reads=[r_wtok[sub], r_const], writes=[tres])
            P.op("vector", lambda E, tps=tps, sub=sub: E.tensor_copy(
                out=wT, in_=tps),
                reads=[tres], writes=[r_wT[sub]])
        for sub in range(NSUB):
            for half in range(2):
                ps, pres = next_ps()
                P.op("tensor", lambda E, ps=ps, sub=sub, half=half: E.matmul(
                    out=ps, lhsT=wT,
                    rhs=b2r[:, half * 512:half * 512 + 512],
                    start=True, stop=True),
                    reads=[r_wT[sub], r_const], writes=[pres])
                P.op("vector", lambda E, ps=ps, sub=sub, half=half:
                     E.tensor_copy(
                         out=y2acc[:, sub * D + half * 512:
                                   sub * D + half * 512 + 512], in_=ps),
                     reads=[pres, r_out], writes=[r_y2[sub][half]])
        P.filler = []
        for e in range(NE):
            for fb in range(4):
                P.filler.append(lambda e=e, fb=fb: _moe_unit(e, fb))

    def _moe_unit(e, fb):
        if True:
            if True:
                slot1, sl1res = wload(m1_s[e, fb], r_m1_s[e])
                for fc in range(NCH):
                    ps, pres = next_ps()
                    fns = []
                    for dc in range(NCH):
                        fns.append(lambda E, ps=ps, slot1=slot1, dc=dc, fc=fc:
                                   E.matmul(
                                       out=ps,
                                       lhsT=slot1[:, dc * D + fc * 128:
                                                  dc * D + fc * 128 + 128],
                                       rhs=xnfm_ch[dc],
                                       start=(dc == 0), stop=(dc == NCH - 1)))
                    P.group("tensor", fns, reads=[sl1res] + r_xnfm,
                            writes=[pres])
                    P.op("scalar", lambda E, ps=ps, e=e, fb=fb, fc=fc:
                         E.activation(out=y1_ch[fc], in_=ps,
                                      func=ACT.Gelu_apprx_tanh,
                                      bias=b1r[:, e * 32 + fb * 8 + fc:
                                               e * 32 + fb * 8 + fc + 1]),
                         reads=[pres, r_const], writes=[r_y1[fc]])
                slot2, sl2res = wload(m2_s[e, fb], r_m2_s[e])
                for sub in range(NSUB):
                    for half in range(2):
                        ps, pres = next_ps()
                        fns = []
                        for fc in range(NCH):
                            fns.append(lambda E, ps=ps, slot2=slot2, fc=fc,
                                       sub=sub, half=half:
                                       E.matmul(
                                           out=ps,
                                           lhsT=y1_ch[fc][:, sub * 128:
                                                          (sub + 1) * 128],
                                           rhs=slot2[:, fc * D + half * 512:
                                                     fc * D + half * 512 + 512],
                                           start=(fc == 0),
                                           stop=(fc == NCH - 1)))
                        P.group("tensor", fns, reads=[sl2res] + r_y1,
                                writes=[pres])
                        im = nxt("moeps", 2)
                        mp = moeps[:, im * 512:(im + 1) * 512]
                        mpres = r_moeps[im]
                        P.op("scalar", lambda E, ps=ps, mp=mp, sub=sub, e=e:
                             E.activation(out=mp, in_=ps, func=ACT.Identity,
                                          scale=wtok[:, sub * 8 + e:
                                                     sub * 8 + e + 1]),
                             reads=[pres, r_wtok[sub]], writes=[mpres])
                        P.op("gpsimd", lambda E, mp=mp, sub=sub, half=half:
                             E.tensor_add(
                                 out=y2acc[:, sub * D + half * 512:
                                           sub * D + half * 512 + 512],
                                 in0=y2acc[:, sub * D + half * 512:
                                           sub * D + half * 512 + 512],
                                 in1=mp),
                             reads=[mpres, r_y2[sub][half]],
                             writes=[r_y2[sub][half]])

    for t in range(NT if "main" not in skip else 0):
        xv = x_d[t * T:(t + 1) * T, :].rearrange("(s p) f -> p s f", p=128)
        P.op("gpsimd", lambda E, xv=xv: E.dma_start(
            out=stream.rearrange("p (s f) -> p s f", s=NSUB), in_=xv),
            writes=r_stream, dma="xld")
        for sub in range(NSUB):
            emit_ln(stream[:, sub * D:(sub + 1) * D],
                    stream[:, sub * D:(sub + 1) * D],
                    sub, r_stream[sub], r_stream[sub])
            cast_transpose_sub(stream, r_stream, xnfm_ch, r_xnfm, a_tm,
                               r_atm, sub)
        if with_moe:
            emit_moe(t)
        else:
            for sub in range(NSUB):
                for half in range(2):
                    P.op("vector", lambda E, sub=sub, half=half: E.memset(
                        y2acc[:, sub * D + half * 512:
                              sub * D + half * 512 + 512], 0.0),
                        reads=[r_out], writes=[r_y2[sub][half]])
        if with_attn:
            for l in range(L):
                src_ch, src_res = (xnfm_ch, r_xnfm) if l == 0 else (sfm_ch,
                                                                    r_sfm)
                if l > 0:
                    cast_transpose(stream, r_stream, sfm_ch, r_sfm, a_tm,
                                   r_atm)
                emit_attn_layer(l, src_ch, src_res)
        P.flush_filler()
        for sub in range(NSUB):
            if "fin" in skip:
                continue
            P.op("vector", lambda E, sub=sub: E.tensor_add(
                out=y2acc[:, sub * D:(sub + 1) * D],
                in0=y2acc[:, sub * D:(sub + 1) * D],
                in1=stream[:, sub * D:(sub + 1) * D]),
                reads=[r_stream[sub], r_y2[sub][0], r_y2[sub][1]],
                writes=[r_y2[sub][0], r_y2[sub][1]])
            for half in range(2):
                xrv = x_d[t * T + sub * 128:t * T + (sub + 1) * 128,
                          half * 512:half * 512 + 512]
                P.op("gpsimd", lambda E, xrv=xrv: E.dma_start(out=xr, in_=xrv),
                     writes=[r_xr], dma="xrld")
                P.op("vector", lambda E, sub=sub, half=half: E.tensor_add(
                    out=y2acc[:, sub * D + half * 512:
                              sub * D + half * 512 + 512],
                    in0=y2acc[:, sub * D + half * 512:
                              sub * D + half * 512 + 512],
                    in1=xr),
                    reads=[r_xr, r_y2[sub][half]],
                    writes=[r_y2[sub][half]])
        ov = out_d[t * T:(t + 1) * T, :].rearrange("(s p) f -> p s f", p=128)
        P.op("gpsimd", lambda E, ov=ov: E.dma_start(
            out=ov, in_=y2acc.rearrange("p (s f) -> p s f", s=NSUB)),
            reads=[r_y2[s][h] for s in range(NSUB) for h in range(2)],
            writes=[r_out], dma="outst")

    P.q["gpsimd"].append(
        lambda E, s=P.sem["outst"], v=P.cnt["outst"]: E.wait_ge(s, v))

    P.replay()
    return nc


_CACHE = {}


def _get_program(S_len, L, with_moe, detect_races=False):
    key = (S_len, L, with_moe, detect_races)
    if key not in _CACHE:
        _CACHE[key] = build_program(S_len, L, with_moe, detect_races)
    return _CACHE[key]


def make_inputs(ins, S_len=4096, L=3):
    import ml_dtypes
    f32 = lambda a: np.ascontiguousarray(np.asarray(a, dtype=np.float32))
    base = {
        "Wq": f32(ins["L0_attn_Wq"][:L]),
        "Wk": f32(ins["L0_attn_Wk"][:L]),
        "Wv": f32(ins["L0_attn_Wv"][:L]),
        "Wo": f32(ins["L0_attn_Wo"][:L]),
        "bq": f32(ins["L0_attn_bq"][:L]),
        "bk": f32(ins["L0_attn_bk"][:L]),
        "bv": f32(ins["L0_attn_bv"][:L]),
        "bo": f32(ins["L0_attn_bo"][:L]),
        "omega": f32(ins["L0_attn_omega"][:L]),
        "ln1_g": f32(ins["L0_attn_ln1_g"][:L]),
        "ln1_b": f32(ins["L0_attn_ln1_b"][:L]),
        "ln2_g": f32(ins["L0_attn_ln2_g"][:L]),
        "ln2_b": f32(ins["L0_attn_ln2_b"][:L]),
        "W1": f32(ins["L0_attn_W1"][:L]),
        "bf1": f32(ins["L0_attn_bf1"][:L]),
        "W2": f32(ins["L0_attn_W2"][:L]),
        "bf2": f32(ins["L0_attn_bf2"][:L]),
        "gate_w": f32(ins["L0_ffn_gate_w"]),
        "mw1": f32(ins["L0_ffn_w1"]),
        "mb1": f32(ins["L0_ffn_b1"]),
        "mw2": f32(ins["L0_ffn_w2"]),
        "mb2": f32(ins["L0_ffn_b2"]),
        "ident_bf": np.eye(128, dtype=ml_dtypes.bfloat16),
        "ident_f32": np.eye(128, dtype=np.float32),
        "umask": np.triu(np.ones((128, 128), dtype=np.float32)),
    }
    x = f32(ins["L0_attn_x"])
    B = x.shape[0]
    maps = [dict(base, x=np.ascontiguousarray(x[b, :S_len])) for b in range(B)]
    return maps


def kernel(**inputs):
    S_len, L = 4096, 3
    nc = _get_program(S_len, L, True)
    in_maps = make_inputs(inputs, S_len, L)
    res = run_bass_kernel_spmd(nc, in_maps, core_ids=list(range(8)))
    out = np.stack([np.asarray(r["out"], dtype=np.float32)
                    for r in res.results], axis=0)
    return out
